# version 22 (final)
# baseline (speedup 1.0000x reference)
import contextlib
import numpy as np
import ml_dtypes
import concourse.bass as bass
import concourse.mybir as mybir
from concourse.bass_utils import run_bass_kernel_spmd
from concourse.bass_isa import ReduceOp

F32 = mybir.dt.float32
BF16 = mybir.dt.bfloat16
I32 = mybir.dt.int32
AF = mybir.ActivationFunctionType
OP = mybir.AluOpType
AX = mybir.AxisListType

D = 1024
L = 2048
BLOC = 2
NT = 16
DT = 8
NS = 16
II = 2816
IT = 22
CH = 4
NCORE = 8
AW = 16384

ENGS = ("sync", "vector", "scalar", "tensor", "gpsimd")


class Buf:
    __slots__ = ("name", "w", "r")

    def __init__(self, name):
        self.name = name
        self.w = None
        self.r = []


NDL = 12
NDG = 8
GN = 0
GASSIGN = [((i + 1) * GN) // NS - (i * GN) // NS == 1 for i in range(16)]


class Plan:
    def __init__(self):
        self.stream = {e: [] for e in ENGS}
        self.cnt = dict(v=0, s=0, t=0, g=0, c=0)
        for i in range(NDL):
            self.cnt[f"dl{i}"] = 0
        for i in range(NDG):
            self.cnt[f"dg{i}"] = 0
        self.chain = dict(sync="c", vector="v", scalar="s", tensor="t",
                          gpsimd="g")
        self.maxwait = {e: {} for e in ENGS}
        self.pending = {e: [] for e in ENGS}
        self.rr = dict(dl=0, dg=0)

    def _sem_of(self, eng, dma):
        if dma:
            q = "dl" if eng in ("sync", "scalar") else "dg"
            n = NDL if q == "dl" else NDG
            i = self.rr[q]
            self.rr[q] = (i + 1) % n
            return f"{q}{i}"
        return self.chain[eng]

    def fence(self):
        snap = [(s, c) for s, c in self.cnt.items() if c > 0]
        for e in ENGS:
            self.pending[e] = list(snap)

    def emit(self, eng, fn, reads=(), writes=(), dma=False, extra=(),
             wait_reads=None, rec_reads=None):
        waits = {}

        def add(s, c):
            waits[s] = max(waits.get(s, 0), c)

        for b in (reads if wait_reads is None else wait_reads):
            if b.w is not None:
                add(*b.w)
        for b in writes:
            if b.w is not None:
                add(*b.w)
            for tok in b.r:
                add(*tok)
        for tok in extra:
            add(*tok)
        for tok in self.pending[eng]:
            add(*tok)
        self.pending[eng] = []
        mw = self.maxwait[eng]
        wl = []
        for s, c in waits.items():
            if mw.get(s, -1) < c:
                wl.append((s, c))
                mw[s] = c
        sem = self._sem_of(eng, dma)
        amt = 16 if dma else 1
        if dma and self.cnt[sem] > 0 and mw.get(sem, -1) < self.cnt[sem]:
            wl.append((sem, self.cnt[sem]))
            mw[sem] = self.cnt[sem]
        self.cnt[sem] += amt
        tok = (sem, self.cnt[sem])
        self.stream[eng].append((tuple(wl), fn, sem, amt))
        for b in writes:
            b.w = tok
            b.r = []
        for b in (reads if rec_reads is None else rec_reads):
            b.r.append(tok)
        return tok


def build_nc(G_QUANT=True, DEBUG=False):
    nc = bass.Bass("TRN2", target_bir_lowering=False)
    p = Plan()
    em = p.emit

    x_d = nc.dram_tensor("x", [BLOC * L, D], F32, kind="ExternalInput").ap()
    A_d = nc.dram_tensor("A_img", [128, DT * NS], F32, kind="ExternalInput").ap()
    wdt_d = nc.dram_tensor("wdt_img", [128, DT], F32, kind="ExternalInput").ap()
    bdt_d = nc.dram_tensor("bdt_img", [128, DT], F32, kind="ExternalInput").ap()
    wxp_d = nc.dram_tensor("wxp", [32, D], F32, kind="ExternalInput").ap()
    bxp_d = nc.dram_tensor("bxp_img", [32, 1], F32, kind="ExternalInput").ap()
    wout_d = nc.dram_tensor("wout", [D, D], F32, kind="ExternalInput").ap()
    bout_d = nc.dram_tensor("bout_img", [128, DT], F32, kind="ExternalInput").ap()
    clip_d = nc.dram_tensor("clips_img", [1, 2], F32, kind="ExternalInput").ap()
    wg_d = nc.dram_tensor("w_gate", [II, D], F32, kind="ExternalInput").ap()
    wu_d = nc.dram_tensor("w_up", [II, D], F32, kind="ExternalInput").ap()
    wd_d = nc.dram_tensor("w_down", [D, II], F32, kind="ExternalInput").ap()
    out_d = nc.dram_tensor("out", [BLOC * L, D], F32, kind="ExternalOutput").ap()

    wgT_d = nc.dram_tensor("wgT_sp", [128, IT * 1024], BF16).ap()
    wuT_d = nc.dram_tensor("wuT_sp", [128, IT * 1024], BF16).ap()
    wdT_d = nc.dram_tensor("wdT_sp", [128, DT * 2816], BF16).ap()
    woT_d = nc.dram_tensor("woT_sp", [DT * 128, D], BF16).ap()
    x1_d = nc.dram_tensor("x1_sp", [BLOC * L, D], F32).ap()
    xl2_d = nc.dram_tensor("xl2_sp", [BLOC * L, D], F32).ap()
    hraw_d = nc.dram_tensor("hraw_sp", [BLOC * II, L], F32).ap()
    dt_d = nc.dram_tensor("dt_sp", [BLOC * DT * 128, L], F32).ap()
    dtx_d = nc.dram_tensor("dtx_sp", [BLOC * DT * 128, L], F32).ap()
    y_d = nc.dram_tensor("y_sp", [BLOC * DT * 128, L], F32).ap()
    if DEBUG:
        dbg_xln = nc.dram_tensor("dbg_xln", [L, D], F32).ap()
        dbg_bc = nc.dram_tensor("dbg_bc", [32, L], F32).ap()
        dbg_dt = nc.dram_tensor("dbg_dt", [128, L], F32).ap()
        dbg_dtx = nc.dram_tensor("dbg_dtx", [128, L], F32).ap()
        dbg_dA = nc.dram_tensor("dbg_dA", [128, L], F32).ap()
        dbg_stt = nc.dram_tensor("dbg_stt", [128, L], F32).ap()
        dbg_y = nc.dram_tensor("dbg_y", [DT * 128, L], F32).ap()
        dbg_attn = nc.dram_tensor("dbg_attn", [DT * 128, L], F32).ap()
        dbg_dB = nc.dram_tensor("dbg_dB", [128, L], F32).ap()
        dbg_sel = nc.dram_tensor("dbg_sel", [32, 32 * 128], BF16).ap()
        dbg_bchi = nc.dram_tensor("dbg_bchi", [32, L], BF16).ap()
        dbg_bclo = nc.dram_tensor("dbg_bclo", [32, L], BF16).ap()
        dbg_bclof = nc.dram_tensor("dbg_bclof", [32, L], F32).ap()
        dbg_bchif = nc.dram_tensor("dbg_bchif", [32, L], F32).ap()
    cc1_in = nc.dram_tensor("cc1_in", [1, 1], F32)
    cc1_out = nc.dram_tensor("cc1_out", [1, 1], F32)
    cc2_in = nc.dram_tensor("cc2_in", [1, 1], F32)
    cc2_out = nc.dram_tensor("cc2_out", [1, 1], F32)

    st_ = contextlib.ExitStack()
    E = st_.enter_context

    def sb(name, shape, dtype=F32):
        return E(nc.sbuf_tensor(name, shape, dtype))

    identf = sb("identf", [128, 128], F32)
    identb = sb("identb", [128, 128], BF16)
    ones_f = sb("ones_f", [1, 128], F32)
    ones_b = sb("ones_b", [1, 128], BF16)
    A_sb = sb("A_sb", [128, DT * NS], F32)
    wdt_sb = sb("wdt_sb", [128, DT], F32)
    bdt_sb = sb("bdt_sb", [128, DT], F32)
    bout_sb = sb("bout_sb", [128, DT], F32)
    bxp_sb = sb("bxp_sb", [32, 1], F32)
    clip_sb = sb("clip_sb", [1, 2], F32)
    clipb_sb = sb("clipb_sb", [128, 2], F32)
    sig_sb = sb("sig_sb", [128, 2], F32)
    wxpT_sb = sb("wxpT", [128, DT * 32], BF16)
    wgs_sb = sb("wgs", [128, IT], F32)
    wus_sb = sb("wus", [128, IT], F32)
    wds_sb = sb("wds", [128, DT], F32)
    sgA_sb = sb("sgA", [128, IT], F32)
    suA_sb = sb("suA", [128, IT], F32)
    sdA_sb = sb("sdA", [128, DT], F32)
    gmx_sb = sb("gmx", [128, 4], F32)
    scal_sb = sb("scal", [128, 8], F32)
    stat_sb = sb("stat", [128, 12], F32)
    wsc_sb = sb("wsc", [128, 16], F32)
    wdisc_sb = sb("wdisc", [128, 16], F32)
    konst_sb = sb("konst", [128, 4], F32)
    selb_sb = sb("selb_sb", [32, 32 * 128], BF16)

    AR1 = sb("AR1", [128, AW], F32)
    AR2 = sb("AR2", [128, AW], F32)
    AR3 = sb("AR3", [128, AW], F32)
    AR1b = AR1.ap().bitcast(BF16)
    AR2b = AR2.ap().bitcast(BF16)
    AR3b = AR3.ap().bitcast(BF16)
    AR1i = AR1.ap().bitcast(I32)
    AR2i = AR2.ap().bitcast(I32)

    pb0 = E(nc.psum_tensor("pb0", [128, 512], F32))
    pb1 = E(nc.psum_tensor("pb1", [128, 512], F32))
    pc0 = E(nc.psum_tensor("pc0", [128, 512], F32))
    pc1 = E(nc.psum_tensor("pc1", [128, 512], F32))
    psY = E(nc.psum_tensor("psY", [128, 2048], F32))
    PB = [pb0, pb1]
    PC = [pc0, pc1]

    sems = {}
    semnames = ["v", "s", "t", "g", "c"] + [f"dl{i}" for i in range(NDL)] + \
        [f"dg{i}" for i in range(NDG)]
    for s in semnames:
        sems[s] = E(nc.semaphore("sem_" + s))

    B_ = {}

    def buf(name):
        if name not in B_:
            B_[name] = Buf(name)
        return B_[name]

    bPB = [buf("pb0"), buf("pb1")]
    bPC = [buf("pc0"), buf("pc1")]
    bPY = [buf(f"psY{c}") for c in range(CH)]

    def ld(dst_ap, src_ap, wbuf, rbufs=(), eng="sync"):
        return em(eng, lambda e, d=dst_ap, s=src_ap: e.dma_start(out=d, in_=s),
                  reads=rbufs, writes=(wbuf,), dma=True)

    def stdma(dst_ap, src_ap, rbuf, wbuf, eng="gpsimd"):
        return em(eng, lambda e, d=dst_ap, s=src_ap: e.dma_start(out=d, in_=s),
                  reads=(rbuf,), writes=(wbuf,), dma=True)

    bid = buf("identf")
    em("gpsimd", lambda g: g.memset(identf[:, :], 0.0), writes=(bid,))
    em("gpsimd", lambda g: g.affine_select(
        out=identf[:, :], in_=identf[:, :], compare_op=OP.not_equal,
        fill=1.0, base=0, pattern=[[-1, 128]], channel_multiplier=1),
        reads=(bid,), writes=(bid,))
    bidb = buf("identb")
    em("gpsimd", lambda g: g.tensor_copy(identb[:, :], identf[:, :]),
       reads=(bid,), writes=(bidb,))
    bonef = buf("ones_f")
    em("gpsimd", lambda g: g.memset(ones_f[:, :], 1.0), writes=(bonef,))
    boneb = buf("ones_b")
    em("gpsimd", lambda g: g.memset(ones_b[:, :], 1.0), writes=(boneb,))
    bkon = buf("konst")
    em("gpsimd", lambda g: g.memset(konst_sb[:, 0:1], 1e-5), writes=(bkon,))
    em("gpsimd", lambda g: g.memset(konst_sb[:, 1:2], -1.0), writes=(bkon,))
    em("gpsimd", lambda g: g.memset(konst_sb[:, 2:3], -0.5), writes=(bkon,))
    em("gpsimd", lambda g: g.memset(konst_sb[:, 3:4], 1.0), writes=(bkon,))
    bsel = buf("sel")
    selftmp = AR1.ap()[0:32, 4096:8192]
    em("gpsimd", lambda g: g.memset(selftmp, 0.0), writes=(bsel,))
    em("gpsimd", lambda g: g.affine_select(
        out=selftmp.rearrange("k (n q) -> k n q", q=128),
        in_=selftmp.rearrange("k (n q) -> k n q", q=128),
        compare_op=OP.not_equal, fill=1.0, base=0,
        pattern=[[-1, 32], [0, 128]], channel_multiplier=1),
        reads=(bsel,), writes=(bsel,))
    em("gpsimd", lambda g: g.tensor_copy(selb_sb[:, :], selftmp),
       reads=(bsel,), writes=(bsel,))
    if DEBUG:
        stdma(dbg_sel, selb_sb[:, :], bsel, buf("dbg"))

    bA = buf("A_sb")
    ld(A_sb[:, :], A_d, bA)
    em("scalar", lambda s: s.activation(A_sb[:, :], A_sb[:, :], AF.Exp),
       reads=(bA,), writes=(bA,))
    em("vector", lambda v: v.tensor_scalar(A_sb[:, :], A_sb[:, :], -1.0, None,
                                           OP.mult), reads=(bA,), writes=(bA,))
    bwdt = buf("wdt"); ld(wdt_sb[:, :], wdt_d, bwdt)
    bbdt = buf("bdt"); ld(bdt_sb[:, :], bdt_d, bbdt)
    bbout = buf("bout"); ld(bout_sb[:, :], bout_d, bbout)
    bbxp = buf("bxp"); ld(bxp_sb[:, :], bxp_d, bbxp)
    bclip = buf("clip"); ld(clip_sb[:, :], clip_d, bclip)
    bclipb = buf("clipb")
    em("tensor", lambda t: t.matmul(pb0[:, 0:2], ones_f[:, :], clip_sb[:, :],
                                    start=True, stop=True),
       reads=(bonef, bclip), writes=(bPB[0],))
    em("scalar", lambda s: s.activation(clipb_sb[:, :], pb0[:, 0:2],
                                        AF.Identity),
       reads=(bPB[0],), writes=(bclipb,))
    bsig = buf("sig")
    em("scalar", lambda s: s.activation(sig_sb[:, :], clipb_sb[:, :], AF.Exp,
                                        scale=konst_sb[:, 1:2]),
       reads=(bclipb, bkon), writes=(bsig,))
    em("vector", lambda v: v.tensor_scalar(sig_sb[:, :], sig_sb[:, :], 1.0,
                                           None, OP.add),
       reads=(bsig,), writes=(bsig,))
    em("vector", lambda v: v.reciprocal(sig_sb[:, :], sig_sb[:, :]),
       reads=(bsig,), writes=(bsig,))
    bgmx = buf("gmx")
    em("vector", lambda v: v.memset(gmx_sb[:, :], 0.0), writes=(bgmx,))

    bwxpf = buf("wxpf")
    wxpf = AR1.ap()[0:32, 0:D]
    ld(wxpf, wxp_d, bwxpf)
    bwxpT = buf("wxpT")
    for j in range(DT):
        pj = PB[j % 2]
        bj = bPB[j % 2]
        em("tensor", lambda t, j=j, pj=pj: t.transpose(
            pj[0:128, 0:32], wxpf[:, 128 * j:128 * (j + 1)],
            identf[0:32, 0:32]), reads=(bwxpf, bid), writes=(bj,))
        em("scalar", lambda s, j=j, pj=pj: s.activation(
            wxpT_sb[:, 32 * j:32 * (j + 1)], pj[0:128, 0:32], AF.Identity),
            reads=(bj,), writes=(bwxpT,))

    bwtmp = buf("wtmp")
    bwoas = buf("wo_as")
    wtmp = AR2.ap()[:, 0:D]
    for r in range(DT):
        ld(wtmp, wout_d[128 * r:128 * (r + 1), :], bwtmp)
        for j in range(DT):
            pj = PB[j % 2]
            bj = bPB[j % 2]
            em("tensor", lambda t, j=j, pj=pj: t.transpose(
                pj[:, 0:128], wtmp[:, 128 * j:128 * (j + 1)], identf[:, :]),
                reads=(bwtmp, bid), writes=(bj,))
            em("scalar", lambda s, r=r, j=j, pj=pj: s.activation(
                AR3b[:, 22528 + 1024 * j + 128 * r:
                     22528 + 1024 * j + 128 * (r + 1)],
                pj[:, 0:128], AF.Identity), reads=(bj,), writes=(bwoas,))
    for j in range(DT):
        stdma(woT_d[128 * j:128 * (j + 1), :],
              AR3b[:, 22528 + 1024 * j:22528 + 1024 * (j + 1)], bwoas,
              buf("woT_spill"))

    p.fence()
    bwsc6 = [buf(f"wsc{k}") for k in range(6)]
    bwsc = buf("wsc_wd")
    bwdisc = buf("wdisc")

    def mk_sets(regions, width):
        out = []
        for k, (AR_, ARi_, o) in enumerate(regions):
            bin_ = buf(f"ch_in_{id(regions)}_{k}")
            s = dict(inp=AR_.ap()[:, o:o + width],
                     a=AR_.ap()[:, o + width:o + 2 * width],
                     sg=AR_.ap()[:, o + 2 * width:o + 3 * width],
                     u0=AR_.ap()[:, o + 3 * width:o + 4 * width],
                     i32=ARi_[:, o:o + width],
                     ng=AR_.ap()[:, o + 4 * width:o + 5 * width])
            bs = dict(a=buf("ch_a"), sg=buf("ch_sg"), u0=buf("ch_u0"),
                      i32=bin_, ng=buf("ch_ng"))
            out.append((s, bin_, bs))
        return out

    def chain_pipe(sets, units, conv_eng, rf2_eng, after_tail=None,
                   front_off=1):
        NU = len(units)

        def pre(u):
            scr, bin_, bscr = sets[u % len(sets)]
            un = units[u]
            ld(scr["inp"][:, 0:un["n"]], un["src"], bin_,
               rbufs=un.get("ldr", ()))
            if "pre" in un:
                un["pre"](scr, bin_)

        def front(u):
            scr, bin_, bscr = sets[u % len(sets)]
            un = units[u]
            N = un["n"]
            isc_ap, bisc = un["isc"]
            em("scalar", lambda s, scr=scr, N=N, isc_ap=isc_ap: s.activation(
                scr["a"][:, 0:N], scr["inp"][:, 0:N], AF.Abs, scale=isc_ap),
               reads=(bin_, bisc), writes=(bscr["a"],))
            em("scalar", lambda s, scr=scr, N=N: s.activation(
                scr["sg"][:, 0:N], scr["inp"][:, 0:N], AF.Sign),
               reads=(bin_,), writes=(bscr["sg"],))

        def mid(u):
            scr, bin_, bscr = sets[u % len(sets)]
            N = units[u]["n"]
            em("vector", lambda v, scr=scr, N=N: v.tensor_scalar(
                scr["u0"][:, 0:N], scr["a"][:, 0:N], 100000.0, 0.5, OP.min,
                OP.add), reads=(bscr["a"],), writes=(bscr["u0"],))
            if conv_eng == "vector":
                em("vector", lambda v, scr=scr, N=N: v.tensor_copy(
                    scr["i32"][:, 0:N], scr["u0"][:, 0:N]),
                   reads=(bscr["u0"],), writes=(bscr["i32"],))
                em("vector", lambda v, scr=scr, N=N: v.tensor_copy(
                    scr["a"][:, 0:N], scr["i32"][:, 0:N]),
                   reads=(bscr["i32"],), writes=(bscr["a"],))
            else:
                em("scalar", lambda s, scr=scr, N=N: s.copy(
                    scr["i32"][:, 0:N], scr["u0"][:, 0:N]),
                   reads=(bscr["u0"],), writes=(bscr["i32"],))
                em("scalar", lambda s, scr=scr, N=N: s.copy(
                    scr["a"][:, 0:N], scr["i32"][:, 0:N]),
                   reads=(bscr["i32"],), writes=(bscr["a"],))
            em("vector", lambda v, scr=scr, N=N: v.tensor_tensor(
                scr["ng"][:, 0:N], scr["u0"][:, 0:N], scr["a"][:, 0:N],
                OP.is_lt), reads=(bscr["u0"], bscr["a"]),
                writes=(bscr["ng"],))
            em("vector", lambda v, scr=scr, N=N: v.scalar_tensor_tensor(
                scr["u0"][:, 0:N], scr["a"][:, 0:N], 1.0, scr["ng"][:, 0:N],
                OP.mult, OP.subtract),
               reads=(bscr["a"], bscr["ng"]), writes=(bscr["u0"],))

        def tail(u):
            scr, bin_, bscr = sets[u % len(sets)]
            un = units[u]
            N = un["n"]
            em(rf2_eng, lambda e, scr=scr, N=N: e.tensor_tensor(
                scr["ng"][:, 0:N], scr["u0"][:, 0:N], scr["sg"][:, 0:N],
                OP.mult), reads=(bscr["u0"], bscr["sg"]),
                writes=(bscr["ng"],))
            out_ap, bout_ = un["out"](scr, bscr)
            em("vector", lambda v, scr=scr, N=N, out_ap=out_ap:
               v.tensor_scalar(out_ap, scr["ng"][:, 0:N], 127.0, -128.0,
                               OP.min, OP.max),
               reads=(bscr["ng"],), writes=(bout_,))
            if "tail" in un:
                un["tail"](scr, bin_, bscr)

        def tail2(u):
            scr, bin_, bscr = sets[u % len(sets)]
            un = units[u]
            if "tail2" in un:
                un["tail2"](scr, bin_, bscr)

        for i in range(NU + 4):
            if i < NU:
                pre(i)
                if front_off == 0:
                    front(i)
            if front_off == 1 and 1 <= i and i - 1 < NU:
                front(i - 1)
            if 2 <= i and i - 2 < NU:
                mid(i - 2)
            if 3 <= i and i - 3 < NU:
                tail(i - 3)
                if after_tail is not None:
                    after_tail(i - 3)
            if 4 <= i:
                tail2(i - 4)

    wq3 = mk_sets([(AR1, AR1i, 0), (AR1, AR1i, 5120), (AR1, AR1i, 10240),
                   (AR2, AR2i, 0), (AR2, AR2i, 5120), (AR2, AR2i, 10240)],
                  1024)
    basm = buf("wq_asm")
    bwgs = buf("wgs"); bwus = buf("wus"); bwds = buf("wds")

    def wq_pre(scl_sb, scl_buf, tt, iscc, bws):
        def _pre(scr, bin_, tt=tt, iscc=iscc, bws=bws):
            em("vector", lambda v, scr=scr: v.tensor_reduce(
                wsc_sb[:, iscc:iscc + 1], scr["inp"][:, 0:1024], AX.X,
                OP.max, apply_absolute_value=True),
               reads=(bin_,), writes=(bws,))
            em("vector", lambda v: v.tensor_scalar(
                scl_sb[:, tt:tt + 1], wsc_sb[:, iscc:iscc + 1],
                float(np.float32(1.0 / 127.0)), None, OP.mult),
                reads=(bws,), writes=(scl_buf,))
            em("vector", lambda v: v.reciprocal(
                wsc_sb[:, iscc + 1:iscc + 2], wsc_sb[:, iscc:iscc + 1]),
               reads=(bws,), writes=(bws,))
            em("vector", lambda v: v.tensor_scalar(
                wsc_sb[:, iscc + 1:iscc + 2], wsc_sb[:, iscc + 1:iscc + 2],
                127.0, None, OP.mult),
                reads=(bws,), writes=(bws,))
        return _pre

    def wq_tail(u, nblk):
        def _tail(scr, bin_, bscr, u=u, nblk=nblk):
            PJ = PB if u % 2 == 0 else PC
            bPJ = bPB if u % 2 == 0 else bPC
            for ll in range(nblk):
                g = ll // 4
                em("tensor", lambda t, ll=ll, pj=PJ[g], scr=scr: t.transpose(
                    pj[:, 128 * (ll % 4):128 * (ll % 4) + 128],
                    scr["sg"][:, 128 * ll:128 * (ll + 1)],
                    identf[:, :]), reads=(bscr["sg"], bid), writes=(bPJ[g],))
        return _tail

    def wq_evac(u, nblk, asm_col):
        def _ev(scr, bin_, bscr, u=u, nblk=nblk, asm_col=asm_col):
            PJ = PB if u % 2 == 0 else PC
            bPJ = bPB if u % 2 == 0 else bPC
            for g in range((nblk + 3) // 4):
                w = 128 * min(4, nblk - 4 * g)
                em("scalar", lambda s, g=g, w=w, pj=PJ[g], asm_col=asm_col:
                   s.activation(
                       AR3b[:, asm_col + 512 * g:asm_col + 512 * g + w],
                       pj[:, 0:w], AF.Identity), reads=(bPJ[g],),
                   writes=(basm,))
        return _ev

    def wq_out(N):
        return lambda scr, bscr: (scr["sg"][:, 0:N], bscr["sg"])

    for src_d, scl_sb, scl_buf, dstT_d in (
            (wg_d, wgs_sb, bwgs, wgT_d), (wu_d, wus_sb, bwus, wuT_d)):
        units = []
        for tt in range(IT):
            sp = tt % 6
            units.append(dict(
                src=src_d[128 * tt:128 * (tt + 1), :], n=1024,
                isc=(wsc_sb[:, 2 * sp + 1:2 * sp + 2], bwsc6[sp]),
                pre=wq_pre(scl_sb, scl_buf, tt, 2 * sp, bwsc6[sp]),
                out=wq_out(1024),
                tail=wq_tail(tt, DT), tail2=wq_evac(tt, DT, 1024 * tt)))
        chain_pipe(wq3, units, "vector", "vector", front_off=0)
        stdma(dstT_d[:, :], AR3b[:, 0:IT * 1024], basm, buf("wqT_spill"))

    for tt in range(DT):
        for cc in range(3):
            w = 1024 if cc < 2 else 768
            scr, bin_, bscr = wq3[(3 * tt + cc) % len(wq3)]
            ld(scr["inp"][:, 0:w],
               wd_d[128 * tt:128 * (tt + 1), 1024 * cc:1024 * cc + w], bin_)
            if cc == 0:
                em("vector", lambda v, scr=scr, w=w, tt=tt: v.tensor_reduce(
                    wdisc_sb[:, 2 * tt:2 * tt + 1], scr["inp"][:, 0:w], AX.X,
                    OP.max, apply_absolute_value=True),
                   reads=(bin_,), writes=(bwdisc,))
            else:
                tc = 12
                em("vector", lambda v, scr=scr, w=w, tc=tc: v.tensor_reduce(
                    wsc_sb[:, tc:tc + 1], scr["inp"][:, 0:w], AX.X,
                    OP.max, apply_absolute_value=True),
                   reads=(bin_,), writes=(bwsc,))
                em("vector", lambda v, tt=tt, tc=tc: v.tensor_tensor(
                    wdisc_sb[:, 2 * tt:2 * tt + 1],
                    wdisc_sb[:, 2 * tt:2 * tt + 1], wsc_sb[:, tc:tc + 1],
                    OP.max), reads=(bwdisc, bwsc), writes=(bwdisc,))
    for tt in range(DT):
        em("vector", lambda v, tt=tt: v.tensor_scalar(
            wds_sb[:, tt:tt + 1], wdisc_sb[:, 2 * tt:2 * tt + 1],
            float(np.float32(1.0 / 127.0)), None, OP.mult),
            reads=(bwdisc,), writes=(bwds,))
        em("vector", lambda v, tt=tt: v.reciprocal(
            wdisc_sb[:, 2 * tt + 1:2 * tt + 2],
            wdisc_sb[:, 2 * tt:2 * tt + 1]),
           reads=(bwdisc,), writes=(bwdisc,))
        em("vector", lambda v, tt=tt: v.tensor_scalar(
            wdisc_sb[:, 2 * tt + 1:2 * tt + 2],
            wdisc_sb[:, 2 * tt + 1:2 * tt + 2], 127.0, None, OP.mult),
            reads=(bwdisc,), writes=(bwdisc,))
    units = []
    for u0_, (tt, cc) in enumerate([(tt, cc) for tt in range(DT)
                                    for cc in range(3)]):
        w = 1024 if cc < 2 else 768
        units.append(dict(
            src=wd_d[128 * tt:128 * (tt + 1), 1024 * cc:1024 * cc + w],
            n=w, isc=(wdisc_sb[:, 2 * tt + 1:2 * tt + 2], bwdisc),
            out=wq_out(w),
            tail=wq_tail(u0_, w // 128),
            tail2=wq_evac(u0_, w // 128, 2816 * tt + 128 * 8 * cc)))
    chain_pipe(wq3, units, "vector", "vector", front_off=0)
    stdma(wdT_d[:, :], AR3b[:, 0:DT * 2816], basm, buf("wqT_spill"))

    p.fence()

    xlnT = AR1.ap()
    xfm = AR2.ap()
    xbf = AR3b
    sbc = AR3.ap()[:, 14336:16384]
    dtv = [AR1.ap()[:, 0:2048], AR1.ap()[:, 2048:4096]]
    dtxv = [AR1.ap()[:, 4096:6144], AR1.ap()[:, 6144:8192]]
    bcv = AR3.ap()[0:32, 8192:10240]
    bchiv = AR3b[0:32, 20480:22528]
    bclov = AR3b[0:32, 22528:24576]
    bclofv = AR3.ap()[0:32, 12288:14336]
    cacheB = [AR1.ap()[:, 2048 * k:2048 * (k + 1)] for k in range(4)]
    cacheC = [AR1.ap()[:, 8192 + 2048 * k:8192 + 2048 * (k + 1)]
              for k in range(4)]
    dts = [AR3.ap()[:, 0:2048], AR3.ap()[:, 2048:4096]]
    dtxs = [AR3.ap()[:, 4096:6144], AR3.ap()[:, 6144:8192]]
    dAv = [AR3.ap()[:, 12288:14336], AR3.ap()[:, 14336:16384]]
    sttv = [AR2.ap()[:, 0:2048], AR2.ap()[:, 2048:4096]]
    dBv = [AR2.ap()[:, 4096:6144], AR2.ap()[:, 6144:8192]]
    prodv = [AR2b[:, 16384:18432], AR2b[:, 18432:20480]]
    ybufv = [AR2.ap()[:, 10240:12288], AR2.ap()[:, 12288:14336]]
    ybf = AR1b
    attn = AR2.ap()
    xTv2 = [AR3.ap()[:, 0:1024], AR3.ap()[:, 1024:2048]]
    xT2v2 = [AR3.ap()[:, 10240:11264], AR3.ap()[:, 11264:12288]]
    x1Tv = [AR3.ap()[:, 2048:3072], AR3.ap()[:, 3072:4096]]
    xl2Tv = [AR3.ap()[:, 4096:5120], AR3.ap()[:, 5120:6144]]
    woTl = AR3b[:, 12288:20480]

    bxlnT = [buf(f"xlnT{t}") for t in range(NT)]
    bxfm = [buf(f"xfm{j}") for j in range(DT)]
    bxbf = [buf(f"xbf{j}") for j in range(DT)]
    bstat = buf("stat")
    bsbc = buf("sbc")
    bdt = [buf("dt0"), buf("dt1")]
    bdtx = [buf("dtx0"), buf("dtx1")]
    bdts = [buf("dts0"), buf("dts1")]
    bdtxs = [buf("dtxs0"), buf("dtxs1")]
    bdA = [buf("dA0"), buf("dA1")]
    bdB = [buf("dB0"), buf("dB1")]
    bstt = [buf("stt0"), buf("stt1")]
    bcB = [buf(f"cacheB{k}") for k in range(4)]
    bcC = [buf(f"cacheC{k}") for k in range(4)]
    bybuf = [buf("ybuf0"), buf("ybuf1")]
    bdtd = [buf(f"dtd{b}_{j}") for b in range(BLOC) for j in range(DT)]
    bdtxd = [buf(f"dtxd{b}_{j}") for b in range(BLOC) for j in range(DT)]
    byd = [buf(f"yd{b}_{j}") for b in range(BLOC) for j in range(DT)]
    bbc = buf("bc"); bbchi = buf("bchi"); bbclo = buf("bclo")
    bbclof = buf("bclof")
    bprod = [buf("prod0"), buf("prod1")]
    bybf = [buf(f"ybf{j}") for j in range(DT)]
    battn = [buf(f"attn{m}") for m in range(DT)]
    bxTp = [buf("xT0"), buf("xT1")]
    bxT2p = [buf("xT20"), buf("xT21")]
    bstatL = [buf("statL0"), buf("statL1")]
    bx1T = [buf("x1T0"), buf("x1T1")]
    bxl2T = [buf("xl2T0"), buf("xl2T1")]
    bwoTl = buf("woTl")

    def ln_tile(src_buf_ap, bsrc, dst_ap, bdst, pp=0):
        o = 8 * pp
        bst = bstatL[pp]
        xT2v = xT2v2[pp]
        bxT2 = bxT2p[pp]
        em("vector", lambda v: v.reduce_sum(stat_sb[:, o:o + 1], src_buf_ap,
                                            AX.X),
           reads=(bsrc,), writes=(bst,))
        em("scalar", lambda s: s.activation(xT2v, src_buf_ap, AF.Square),
           reads=(bsrc,), writes=(bxT2,))
        em("vector", lambda v: v.reduce_sum(stat_sb[:, o + 1:o + 2], xT2v,
                                            AX.X),
           reads=(bxT2,), writes=(bst,))
        em("vector", lambda v: v.tensor_scalar(
            stat_sb[:, o:o + 2], stat_sb[:, o:o + 2],
            float(np.float32(1.0 / D)), None,
            OP.mult), reads=(bst,), writes=(bst,))
        em("vector", lambda v: v.tensor_tensor(
            stat_sb[:, o + 2:o + 3], stat_sb[:, o:o + 1],
            stat_sb[:, o:o + 1], OP.mult),
            reads=(bst,), writes=(bst,))
        em("vector", lambda v: v.tensor_tensor(
            stat_sb[:, o + 2:o + 3], stat_sb[:, o + 1:o + 2],
            stat_sb[:, o + 2:o + 3], OP.subtract),
            reads=(bst,), writes=(bst,))
        em("scalar", lambda s: s.activation(
            stat_sb[:, o + 3:o + 4], stat_sb[:, o + 2:o + 3], AF.Ln,
            bias=konst_sb[:, 0:1]),
            reads=(bst, bkon), writes=(bst,))
        em("scalar", lambda s: s.activation(
            stat_sb[:, o + 3:o + 4], stat_sb[:, o + 3:o + 4], AF.Exp,
            scale=konst_sb[:, 2:3]),
            reads=(bst, bkon), writes=(bst,))
        em("vector", lambda v: v.tensor_scalar(
            dst_ap, src_buf_ap, stat_sb[:, o:o + 1], stat_sb[:, o + 3:o + 4],
            OP.subtract, OP.mult), reads=(bsrc, bst), writes=(bdst,))

    for b in range(BLOC):
        p.fence()
        for tt in range(NT):
            row0 = b * L + 128 * tt
            tp = tt % 2
            ld(xTv2[tp], x_d[row0:row0 + 128, :], bxTp[tp])
            ln_tile(xTv2[tp], bxTp[tp], xlnT[:, D * tt:D * (tt + 1)],
                    bxlnT[tt], pp=tp)
        if DEBUG and b == 0:
            for tt in range(NT):
                stdma(dbg_xln[128 * tt:128 * (tt + 1), :],
                      xlnT[:, D * tt:D * (tt + 1)], bxlnT[tt], buf("dbg"))
        p.fence()
        for j in range(DT):
            for tt in range(NT):
                pj = PC[tt % 2]
                bj = bPC[tt % 2]
                em("tensor", lambda t, j=j, tt=tt, pj=pj: t.transpose(
                    pj[:, 0:128],
                    xlnT[:, D * tt + 128 * j:D * tt + 128 * (j + 1)],
                    identf[:, :]), reads=(bxlnT[tt], bid), writes=(bj,))
                em("scalar", lambda s, j=j, tt=tt, pj=pj: s.activation(
                    xfm[:, 2048 * j + 128 * tt:2048 * j + 128 * (tt + 1)],
                    pj[:, 0:128], AF.Identity), reads=(bj,), writes=(bxfm[j],))
            em("scalar", lambda s, j=j: s.activation(
                xbf[:, 2048 * j:2048 * (j + 1)],
                xfm[:, 2048 * j:2048 * (j + 1)], AF.Identity),
                reads=(bxfm[j],), writes=(bxbf[j],))
        for c in range(CH):
            pj = PB[c % 2]
            bj = bPB[c % 2]
            for j in range(DT):
                em("tensor", lambda t, c=c, j=j, pj=pj: t.matmul(
                    pj[0:32, :], wxpT_sb[:, 32 * j:32 * (j + 1)],
                    xbf[:, 2048 * j + 512 * c:2048 * j + 512 * (c + 1)],
                    start=(j == 0), stop=(j == DT - 1)),
                    reads=(bwxpT, bxbf[j]), writes=(bj,))
            em("scalar", lambda s, c=c, pj=pj: s.activation(
                bcv[:, 512 * c:512 * (c + 1)], pj[0:32, :], AF.Identity,
                bias=bxp_sb[:, 0:1]), reads=(bj, bbxp), writes=(bbc,))
        em("vector", lambda v: v.tensor_copy(bchiv, bcv), reads=(bbc,),
           writes=(bbchi,))
        em("vector", lambda v: v.tensor_copy(bclofv, bchiv), reads=(bbchi,),
           writes=(bbclof,))
        if DEBUG and b == 0:
            stdma(dbg_bchif, bclofv, bbclof, buf("dbg"))
        em("vector", lambda v: v.tensor_tensor(bclofv, bcv, bclofv,
                                               OP.subtract),
           reads=(bbc, bbclof), writes=(bbclof,))
        if DEBUG and b == 0:
            stdma(dbg_bclof, bclofv, bbclof, buf("dbg"))
        em("vector", lambda v: v.tensor_copy(bclov, bclofv), reads=(bbclof,),
           writes=(bbclo,))
        if DEBUG and b == 0:
            stdma(dbg_bc, bcv, bbc, buf("dbg"))
            stdma(dbg_bchi, bchiv, bbchi, buf("dbg"))
            stdma(dbg_bclo, bclov, bbclo, buf("dbg"))
        p.fence()
        for c in range(CH):
            pj = PC[c % 2]
            bj = bPC[c % 2]
            em("tensor", lambda t, c=c, pj=pj: t.matmul(
                pj[:, :], ones_f[:, :],
                xfm[0:1, 512 * c:512 * (c + 1)], start=True, stop=True),
                reads=(bonef, bxfm[0]), writes=(bj,))
            em("scalar", lambda s, c=c, pj=pj: s.activation(
                sbc[:, 512 * c:512 * (c + 1)], pj[:, :], AF.Identity),
                reads=(bj,), writes=(bsbc,))
        for j in range(DT):
            jp = j % 2
            em("scalar", lambda s, j=j, jp=jp: s.activation(
                dtv[jp], sbc, AF.Exp, scale=wdt_sb[:, j:j + 1],
                bias=bdt_sb[:, j:j + 1]),
                reads=(bsbc, bwdt, bbdt), writes=(bdt[jp],))
            em("scalar", lambda s, jp=jp: s.activation(
                dtv[jp], dtv[jp], AF.Ln, bias=konst_sb[:, 3:4]),
                reads=(bdt[jp], bkon), writes=(bdt[jp],))
            em("vector", lambda v, j=j, jp=jp: v.tensor_tensor(
                dtxv[jp], dtv[jp], xfm[:, 2048 * j:2048 * (j + 1)], OP.mult),
                reads=(bdt[jp], bxfm[j]), writes=(bdtx[jp],))
            if DEBUG and b == 0 and j == 0:
                stdma(dbg_dt, dtv[jp], bdt[jp], buf("dbg"))
                stdma(dbg_dtx, dtxv[jp], bdtx[jp], buf("dbg"))
            r0 = (b * DT + j) * 128
            stdma(dt_d[r0:r0 + 128, :], dtv[jp], bdt[jp], bdtd[b * DT + j])
            stdma(dtx_d[r0:r0 + 128, :], dtxv[jp], bdtx[jp], bdtxd[b * DT + j])
        p.fence()

        def build_cache_nn(nq, nn):
            n = 4 * nq + nn
            for src_row, dst, bdst in ((n, cacheB[nn], bcB[nn]),
                                       (NS + n, cacheC[nn], bcC[nn])):
                for c2 in range(0, CH, 2):
                    for c in (c2, c2 + 1):
                        em("tensor",
                           lambda t, r=src_row, c=c, pj=PB[c % 2]:
                           t.matmul(
                               pj[:, :], selb_sb[:, 128 * r:128 * (r + 1)],
                               bchiv[:, 512 * c:512 * (c + 1)],
                               start=True, stop=False),
                           reads=(bsel, bbchi), writes=(bPB[c % 2],))
                    for c in (c2, c2 + 1):
                        em("tensor",
                           lambda t, r=src_row, c=c, pj=PB[c % 2]:
                           t.matmul(
                               pj[:, :], selb_sb[:, 128 * r:128 * (r + 1)],
                               bclov[:, 512 * c:512 * (c + 1)],
                               start=False, stop=True),
                           reads=(bsel, bbclo), writes=(bPB[c % 2],))
                    for c in (c2, c2 + 1):
                        em("scalar", lambda s, c=c, pj=PB[c % 2], dst=dst:
                           s.activation(dst[:, 512 * c:512 * (c + 1)],
                                        pj[:, :], AF.Identity),
                           reads=(bPB[c % 2],), writes=(bdst,))

        for nn in range(4):
            build_cache_nn(0, nn)
        for nq in range(4):
            for j in range(DT):
                jp = j % 2
                r0 = (b * DT + j) * 128
                ld(dts[jp], dt_d[r0:r0 + 128, :], bdts[jp],
                   rbufs=(bdtd[b * DT + j],))
                ld(dtxs[jp], dtx_d[r0:r0 + 128, :], bdtxs[jp],
                   rbufs=(bdtxd[b * DT + j],))
                def emit_dAdB(nn, j=j, jp=jp, nq=nq):
                    n = 4 * nq + nn
                    npar = nn % 2
                    geng = "gpsimd" if GASSIGN[n] else "vector"
                    em("scalar", lambda s, j=j, n=n, jp=jp, npar=npar:
                       s.activation(dAv[npar], dts[jp], AF.Exp,
                                    scale=A_sb[:, NS * j + n:NS * j + n + 1]),
                       reads=(bdts[jp], bA), writes=(bdA[npar],))
                    em(geng, lambda e, nn=nn, jp=jp, npar=npar: e.tensor_tensor(
                        dBv[npar], dtxs[jp], cacheB[nn], OP.mult),
                        reads=(bdtxs[jp], bcB[nn]), writes=(bdB[npar],))

                emit_dAdB(0)
                emit_dAdB(1)
                for nn in range(4):
                    n = 4 * nq + nn
                    npar = nn % 2
                    geng = "gpsimd" if GASSIGN[n] else "vector"
                    em("vector", lambda v, npar=npar: v.tensor_tensor_scan(
                        sttv[npar], dAv[npar], dBv[npar], 0.0, OP.mult,
                        OP.add),
                        reads=(bdA[npar], bdB[npar]), writes=(bstt[npar],))
                    if DEBUG and b == 0 and j == 0 and n == 0:
                        stdma(dbg_dA, dAv[npar], bdA[npar], buf("dbg"))
                        stdma(dbg_stt, sttv[npar], bstt[npar], buf("dbg"))
                        stdma(dbg_dB, dBv[npar], bdB[npar], buf("dbg"))
                    if nn + 2 < 4:
                        emit_dAdB(nn + 2)
                    em(geng, lambda e, nn=nn, npar=npar: e.tensor_tensor(
                        prodv[npar], sttv[npar], cacheC[nn], OP.mult),
                        reads=(bstt[npar], bcC[nn]), writes=(bprod[npar],))
                    for c in range(CH):
                        em("tensor", lambda t, nn=nn, c=c, npar=npar: t.matmul(
                            psY[:, 512 * c:512 * (c + 1)], identb[:, :],
                            prodv[npar][:, 512 * c:512 * (c + 1)],
                            start=(nn == 0), stop=(nn == 3)),
                            reads=(bidb, bprod[npar]), writes=(bPY[c],))
                    if j == DT - 1 and nq < 3:
                        build_cache_nn(nq + 1, nn)
                em("scalar", lambda s, jp=jp: s.activation(
                    ybufv[jp], psY[:, :], AF.Identity),
                    reads=tuple(bPY), writes=(bybuf[jp],))
                em("gpsimd", lambda g, b=b, j=j, jp=jp, nq=nq: g.dma_start(
                    out=y_d[(b * DT + j) * 128:(b * DT + j) * 128 + 128, :],
                    in_=ybufv[jp],
                    **({} if nq == 0 else dict(accum_op=OP.add))),
                    reads=(bybuf[jp],), writes=(byd[b * DT + j],), dma=True)
        p.fence()
        for j in range(DT):
            r0 = (b * DT + j) * 128
            em("gpsimd", lambda g, j=j, r0=r0: g.dma_start(
                out=ybf[:, 2048 * j:2048 * (j + 1)],
                in_=y_d[r0:r0 + 128, :]),
                reads=(byd[b * DT + j],), writes=(bybf[j],), dma=True)
        for j in range(DT):
            ld(AR3b[:, 12288 + 1024 * j:12288 + 1024 * (j + 1)],
               woT_d[128 * j:128 * (j + 1), :], bwoTl)
        o_in = (bwoTl,) + tuple(bybf)
        ogroups = [(m, c) for m in range(DT) for c in range(CH)]
        for gi in range(0, len(ogroups), 2):
            pair = ogroups[gi:gi + 2]
            PJ = PB if (gi // 2) % 2 == 0 else PC
            bPJ = bPB if (gi // 2) % 2 == 0 else bPC
            for k in range(DT):
                for pi, (m, c) in enumerate(pair):
                    pj = PJ[pi]
                    bj = bPJ[pi]
                    em("tensor", lambda t, m=m, c=c, k=k, pj=pj: t.matmul(
                        pj[:, :],
                        woTl[:, 1024 * k + 128 * m:1024 * k + 128 * (m + 1)],
                        ybf[:, 2048 * k + 512 * c:2048 * k + 512 * (c + 1)],
                        start=(k == 0), stop=(k == DT - 1)),
                        reads=(), writes=(bj,),
                        wait_reads=o_in if gi == 0 and k == 0 and pi == 0
                        else (),
                        rec_reads=o_in if gi == len(ogroups) - 2 and
                        k == DT - 1 and pi == 1 else ())
            for pi, (m, c) in enumerate(pair):
                pj = PJ[pi]
                bj = bPJ[pi]
                em("scalar", lambda s, m=m, c=c, pj=pj: s.activation(
                    attn[:, 2048 * m + 512 * c:2048 * m + 512 * (c + 1)],
                    pj[:, :], AF.Identity, bias=bout_sb[:, m:m + 1]),
                    reads=(bj, bbout), writes=(battn[m],))
        if DEBUG and b == 0:
            for m in range(DT):
                stdma(dbg_attn[128 * m:128 * (m + 1), :],
                      attn[:, 2048 * m:2048 * (m + 1)], battn[m], buf("dbg"))
        for tt in range(NT):
            tp = tt % 2
            row0 = b * L + 128 * tt
            ld(xTv2[tp], x_d[row0:row0 + 128, :], bxTp[tp])
            for m in range(DT):
                pj = PC[m % 2]
                bj = bPC[m % 2]
                em("tensor", lambda t, tt=tt, m=m, pj=pj: t.transpose(
                    pj[:, 0:128],
                    attn[:, 2048 * m + 128 * tt:2048 * m + 128 * (tt + 1)],
                    identf[:, :]), reads=(battn[m], bid), writes=(bj,))
                em("vector", lambda v, m=m, tp=tp, pj=pj: v.tensor_tensor(
                    x1Tv[tp][:, 128 * m:128 * (m + 1)],
                    xTv2[tp][:, 128 * m:128 * (m + 1)], pj[:, 0:128], OP.add),
                    reads=(bxTp[tp], bj), writes=(bx1T[tp],))
            stdma(x1_d[row0:row0 + 128, :], x1Tv[tp], bx1T[tp],
                  buf("x1_spill"))
            ln_tile(x1Tv[tp], bx1T[tp], xl2Tv[tp], bxl2T[tp], pp=tp)
            em("vector", lambda v, tp=tp: v.tensor_reduce(
                stat_sb[:, 4:5], xl2Tv[tp], AX.X, apply_absolute_value=True,
                op=OP.max), reads=(bxl2T[tp],), writes=(bstat,))
            em("vector", lambda v: v.tensor_tensor(
                gmx_sb[:, 0:1], gmx_sb[:, 0:1], stat_sb[:, 4:5], OP.max),
                reads=(bgmx, bstat), writes=(bgmx,))
            stdma(xl2_d[row0:row0 + 128, :], xl2Tv[tp], bxl2T[tp],
                  buf("xl2_spill"))

    em("gpsimd", lambda g: g.tensor_reduce(
        gmx_sb[0:1, 1:2], gmx_sb[:, 0:1], AX.C, OP.max),
        reads=(bgmx,), writes=(bgmx,))
    bcc1 = buf("cc1")
    stdma(cc1_in.ap(), gmx_sb[0:1, 1:2], bgmx, bcc1)
    em("gpsimd", lambda g: g.collective_compute(
        "AllReduce", OP.max, replica_groups=[list(range(NCORE))],
        ins=[cc1_in.ap().opt()], outs=[cc1_out.ap().opt()]),
        reads=(bcc1,), writes=(bcc1,))
    ld(gmx_sb[0:1, 1:2], cc1_out.ap(), bgmx, rbufs=(bcc1,), eng="gpsimd")
    em("tensor", lambda t: t.matmul(pb0[:, 0:1], ones_f[:, :],
                                    gmx_sb[0:1, 1:2], start=True, stop=True),
       reads=(bonef, bgmx), writes=(bPB[0],))
    em("scalar", lambda s: s.activation(gmx_sb[:, 1:2], pb0[:, 0:1],
                                        AF.Identity),
       reads=(bPB[0],), writes=(bgmx,))
    bscal = buf("scal")
    em("vector", lambda v: v.tensor_tensor(scal_sb[:, 0:1], sig_sb[:, 0:1],
                                           gmx_sb[:, 1:2], OP.mult),
       reads=(bsig, bgmx), writes=(bscal,))
    em("vector", lambda v: v.reciprocal(scal_sb[:, 1:2], scal_sb[:, 0:1]),
       reads=(bscal,), writes=(bscal,))
    em("vector", lambda v: v.tensor_scalar(scal_sb[:, 1:2], scal_sb[:, 1:2],
                                           127.0, None, OP.mult),
       reads=(bscal,), writes=(bscal,))
    em("vector", lambda v: v.tensor_scalar(scal_sb[:, 2:3], scal_sb[:, 0:1],
                                           float(np.float32(1.0 / 127.0)), None,
                                           OP.mult),
       reads=(bscal,), writes=(bscal,))
    bsgA = buf("sgA"); bsuA = buf("suA")
    em("vector", lambda v: v.tensor_scalar(sgA_sb[:, :], wgs_sb[:, :],
                                           scal_sb[:, 2:3], None, OP.mult),
       reads=(bwgs, bscal), writes=(bsgA,))
    em("vector", lambda v: v.tensor_scalar(suA_sb[:, :], wus_sb[:, :],
                                           scal_sb[:, 2:3], None, OP.mult),
       reads=(bwus, bscal), writes=(bsuA,))

    p.fence()
    wgT = AR1b
    wuT = AR2b
    xq3 = mk_sets([(AR1, AR1i, 11264), (AR1, AR1i, 13824),
                   (AR2, AR2i, 13312)], 512)
    sgatev = [AR2.ap()[:, 11264:11776], AR2.ap()[:, 11776:12288]]
    hrawcv = [AR2.ap()[:, 12288:12800], AR2.ap()[:, 12800:13312]]

    bwgT = buf("wgT"); bwuT = buf("wuT")
    bxqfm2 = [[buf(f"xqfm{b}_{j}") for j in range(DT)] for b in range(BLOC)]
    bsgate = [buf("sgate0"), buf("sgate1")]
    bhrawc = [buf("hrawc0"), buf("hrawc1")]
    bhspill = {}

    for j in range(DT):
        off = 512 * (j // 4) + 128 * (j % 4)
        ld(AR1b[:, II * j:II * (j + 1)],
           bass.AP(wgT_d.tensor, off, [[IT * 1024, 128], [1024, IT],
                                       [1, 128]]), bwgT)
        ld(AR2b[:, II * j:II * (j + 1)],
           bass.AP(wuT_d.tensor, off, [[IT * 1024, 128], [1024, IT],
                                       [1, 128]]), bwuT)

    def xq_tail(b, tt, h, up):
        def _tail(scr, bin_, bscr):
            pj = PC[up]
            bj = bPC[up]
            for jj in range(4):
                em("tensor", lambda t, jj=jj, pj=pj, scr=scr: t.transpose(
                    pj[:, 128 * jj:128 * (jj + 1)],
                    scr["sg"][:, 128 * jj:128 * (jj + 1)],
                    identf[:, :]), reads=(bscr["sg"], bid), writes=(bj,))
            for jj in range(4):
                j = 4 * h + jj
                em("scalar", lambda s, b=b, j=j, jj=jj, tt=tt, pj=pj:
                   s.activation(
                       AR3b[:, 16384 * b + 2048 * j + 128 * tt:
                            16384 * b + 2048 * j + 128 * (tt + 1)],
                       pj[:, 128 * jj:128 * (jj + 1)], AF.Identity),
                   reads=(bj,), writes=(bxqfm2[b][j],))
        return _tail

    def xq_units(b, tts):
        us = []
        for tt in tts:
            row0 = b * L + 128 * tt
            for h in range(2):
                us.append(dict(
                    src=xl2_d[row0:row0 + 128, 512 * h:512 * (h + 1)],
                    n=512, isc=(scal_sb[:, 1:2], bscal), out=wq_out(512),
                    tail=xq_tail(b, tt, h, (2 * tt + h) % 2)))
        return us

    chain_pipe(xq3, xq_units(0, range(NT)), "vector", "vector")

    UPB = [pb0, pb1]
    bUPB = [bPB[0], bPB[1]]
    for b in range(BLOC):
        bxqfm = bxqfm2[b]
        xo = 16384 * b
        for it in range(IT):
            for c in range(CH):
                g_in = (bwgT,) + tuple(bxqfm)
                u_in = (bwuT,) + tuple(bxqfm)
                pu = UPB[c % 2]; bu = bUPB[c % 2]
                for k in range(DT):
                    em("tensor", lambda t, it=it, c=c, k=k, xo=xo: t.matmul(
                        psY[:, 512 * c:512 * (c + 1)],
                        AR1b[:, II * k + 128 * it:II * k + 128 * (it + 1)],
                        AR3b[:, xo + 2048 * k + 512 * c:
                             xo + 2048 * k + 512 * (c + 1)],
                        start=(k == 0), stop=(k == DT - 1)),
                        reads=(), writes=(bPY[c],),
                        wait_reads=g_in if k == 0 else (),
                        rec_reads=g_in if k == DT - 1 else ())
                    em("tensor", lambda t, it=it, c=c, k=k, pu=pu, xo=xo:
                       t.matmul(
                           pu[:, :],
                           AR2b[:, II * k + 128 * it:II * k + 128 * (it + 1)],
                           AR3b[:, xo + 2048 * k + 512 * c:
                                xo + 2048 * k + 512 * (c + 1)],
                           start=(k == 0), stop=(k == DT - 1)),
                       reads=(), writes=(bu,),
                       wait_reads=u_in if k == 0 else (),
                       rec_reads=u_in if k == DT - 1 else ())
                em("scalar", lambda s, it=it, c=c: s.activation(
                    sgatev[c % 2], psY[:, 512 * c:512 * (c + 1)], AF.Silu,
                    scale=sgA_sb[:, it:it + 1]),
                    reads=(bPY[c], bsgA), writes=(bsgate[c % 2],))
                em("vector", lambda v, it=it, c=c, pu=pu:
                   v.scalar_tensor_tensor(
                       hrawcv[c % 2], pu[:, :],
                       suA_sb[:, it:it + 1], sgatev[c % 2], OP.mult, OP.mult),
                   reads=(bu, bsuA, bsgate[c % 2]), writes=(bhrawc[c % 2],))
                em("vector", lambda v, c=c: v.tensor_reduce(
                    stat_sb[:, 5:6], hrawcv[c % 2], AX.X,
                    apply_absolute_value=True, op=OP.max),
                    reads=(bhrawc[c % 2],), writes=(bstat,))
                em("vector", lambda v: v.tensor_tensor(
                    gmx_sb[:, 2:3], gmx_sb[:, 2:3], stat_sb[:, 5:6], OP.max),
                    reads=(bgmx, bstat), writes=(bgmx,))
                r0 = b * II + 128 * it
                bhspill[(b, it, c)] = buf(f"hsp{b}_{it}_{c}")
                stdma(hraw_d[r0:r0 + 128, 512 * c:512 * (c + 1)],
                      hrawcv[c % 2], bhrawc[c % 2], bhspill[(b, it, c)])
            if b == 0 and it < DT:
                chain_pipe(xq3, xq_units(1, range(2 * it, 2 * it + 2)),
                           "vector", "vector")

    em("gpsimd", lambda g: g.tensor_reduce(
        gmx_sb[0:1, 3:4], gmx_sb[:, 2:3], AX.C, OP.max),
        reads=(bgmx,), writes=(bgmx,))
    bcc2 = buf("cc2")
    stdma(cc2_in.ap(), gmx_sb[0:1, 3:4], bgmx, bcc2)
    em("gpsimd", lambda g: g.collective_compute(
        "AllReduce", OP.max, replica_groups=[list(range(NCORE))],
        ins=[cc2_in.ap().opt()], outs=[cc2_out.ap().opt()]),
        reads=(bcc2,), writes=(bcc2,))
    ld(gmx_sb[0:1, 3:4], cc2_out.ap(), bgmx, rbufs=(bcc2,), eng="gpsimd")
    em("tensor", lambda t: t.matmul(pb0[:, 0:1], ones_f[:, :],
                                    gmx_sb[0:1, 3:4], start=True, stop=True),
       reads=(bonef, bgmx), writes=(bPB[0],))
    em("scalar", lambda s: s.activation(gmx_sb[:, 3:4], pb0[:, 0:1],
                                        AF.Identity),
       reads=(bPB[0],), writes=(bgmx,))
    em("vector", lambda v: v.tensor_tensor(scal_sb[:, 4:5], sig_sb[:, 1:2],
                                           gmx_sb[:, 3:4], OP.mult),
       reads=(bsig, bgmx), writes=(bscal,))
    em("vector", lambda v: v.reciprocal(scal_sb[:, 5:6], scal_sb[:, 4:5]),
       reads=(bscal,), writes=(bscal,))
    em("vector", lambda v: v.tensor_scalar(scal_sb[:, 5:6], scal_sb[:, 5:6],
                                           127.0, None, OP.mult),
       reads=(bscal,), writes=(bscal,))
    em("vector", lambda v: v.tensor_scalar(scal_sb[:, 6:7], scal_sb[:, 4:5],
                                           float(np.float32(1.0 / 127.0)), None,
                                           OP.mult),
       reads=(bscal,), writes=(bscal,))
    bsdA = buf("sdA")
    em("vector", lambda v: v.tensor_scalar(sdA_sb[:, :], wds_sb[:, :],
                                           scal_sb[:, 6:7], None, OP.mult),
       reads=(bwds, bscal), writes=(bsdA,))

    p.fence()
    hq3 = mk_sets([(AR1, AR1i, 11264), (AR1, AR1i, 13824),
                   (AR2, AR2i, 5632), (AR2, AR2i, 8192),
                   (AR2, AR2i, 10752), (AR2, AR2i, 13312)], 512)
    hqc_par = [AR3b, AR2b]
    bhqc2 = [[buf(f"hqc{pp}_{i}") for i in range(IT)] for pp in range(2)]
    dnf = AR3.ap()[:, 5632:9728]
    bdnf = [buf(f"dnf{h}") for h in range(DT)]
    x1rv = [AR3.ap()[:, 9728:10752], AR3.ap()[:, 10752:11776]]
    outTv = [AR3.ap()[:, 11776:12800], AR3.ap()[:, 12800:13824]]
    bx1r = [buf("x1r0"), buf("x1r1")]
    boutT = [buf("outT0"), buf("outT1")]

    bwdT = buf("wdT")
    for it in range(IT):
        ld(AR1b[:, D * it:D * (it + 1)],
           bass.AP(wdT_d.tensor, 128 * it, [[DT * 2816, 128], [2816, DT],
                                            [1, 128]]), bwdT)

    hq_units = []
    for b in range(BLOC):
        for c in range(CH):
            cp = (b * CH + c) % 2
            for it in range(IT):
                src = bass.AP(hraw_d.tensor,
                              (b * II + 128 * it) * L + 512 * c,
                              [[L, 128], [1, 512]])
                hq_units.append(dict(
                    src=src, n=512, isc=(scal_sb[:, 5:6], bscal),
                    ldr=(bhspill[(b, it, c)],),
                    out=(lambda cp=cp, it=it: lambda scr, bscr: (
                        hqc_par[cp][:, 512 * it:512 * (it + 1)],
                        bhqc2[cp][it]))()))

    def emit_b2_mm(b, c):
            cp = (b * CH + c) % 2
            hqc = hqc_par[cp]
            bhqc = bhqc2[cp]
            DNB = [pb0, pb1, pc0, pc1]
            bDNB = [bPB[0], bPB[1], bPC[0], bPC[1]]
            d_in = (bwdT,) + tuple(bhqc)
            for hh2 in range(0, DT, 2):
                for it in range(IT):
                    for hh in (hh2, hh2 + 1):
                        pg = DNB[hh % 4]
                        bg = bDNB[hh % 4]
                        em("tensor", lambda t, hh=hh, it=it, pg=pg, hqc=hqc:
                           t.matmul(
                               pg[:, :],
                               AR1b[:, D * it + 128 * hh:
                                    D * it + 128 * (hh + 1)],
                               hqc[:, 512 * it:512 * (it + 1)],
                               start=(it == 0), stop=(it == IT - 1)),
                           reads=(), writes=(bg,),
                           wait_reads=d_in if hh2 == 0 and it == 0 and
                           hh == hh2 else (),
                           rec_reads=d_in if hh2 == DT - 2 and it == IT - 1
                           and hh == hh2 + 1 else ())
                for hh in (hh2, hh2 + 1):
                    pg = DNB[hh % 4]
                    bg = bDNB[hh % 4]
                    em("scalar", lambda s, hh=hh, pg=pg: s.activation(
                        dnf[:, 512 * hh:512 * (hh + 1)], pg[:, :], AF.Identity,
                        scale=sdA_sb[:, hh:hh + 1]),
                        reads=(bg, bsdA), writes=(bdnf[hh],))

    def emit_b2_out(b, c):
            for tb in range(4):
                tp = tb % 2
                row0 = b * L + 512 * c + 128 * tb
                ld(x1rv[tp], x1_d[row0:row0 + 128, :], bx1r[tp])
                for hh in range(DT):
                    pyc = (tb * DT + hh) % 4
                    em("tensor", lambda t, hh=hh, tb=tb, pyc=pyc: t.transpose(
                        psY[:, 512 * pyc:512 * pyc + 128],
                        dnf[:, 512 * hh + 128 * tb:512 * hh + 128 * (tb + 1)],
                        identf[:, :]), reads=(bdnf[hh], bid),
                        writes=(bPY[pyc],))
                    em("vector", lambda v, hh=hh, tp=tp, pyc=pyc:
                       v.tensor_tensor(
                           outTv[tp][:, 128 * hh:128 * (hh + 1)],
                           x1rv[tp][:, 128 * hh:128 * (hh + 1)],
                           psY[:, 512 * pyc:512 * pyc + 128],
                           OP.add), reads=(bx1r[tp], bPY[pyc]),
                       writes=(boutT[tp],))
                stdma(out_d[row0:row0 + 128, :], outTv[tp], boutT[tp],
                      buf("out_store"))

    def b2_after_tail(u):
        if (u + 1) % IT == 0:
            ch = u // IT
            emit_b2_mm(ch // CH, ch % CH)
        if u % IT == 16 and u // IT >= 1:
            ch = u // IT - 1
            emit_b2_out(ch // CH, ch % CH)

    chain_pipe(hq3, hq_units, "vector", "vector", after_tail=b2_after_tail)
    emit_b2_out((BLOC * CH - 1) // CH, (BLOC * CH - 1) % CH)

    fin = [(s, c) for s, c in p.cnt.items() if c > 0]
    em("gpsimd", lambda g: g.nop(), extra=fin)
    em("sync", lambda e: e.nop(), extra=fin)

    with nc.Block() as block:
        def emit_stream(eng_name):
            def _inner(eng):
                for wl, fn, sem, amt in p.stream[eng_name]:
                    for s, v in wl:
                        eng.wait_ge(sems[s], v)
                    instr = fn(eng)
                    if instr is not None:
                        instr.then_inc(sems[sem], amt)
            return _inner
        block.sync(emit_stream("sync"))
        block.vector(emit_stream("vector"))
        block.scalar(emit_stream("scalar"))
        block.tensor(emit_stream("tensor"))
        block.gpsimd(emit_stream("gpsimd"))

    st_.close()
    return nc


def kernel(L0_attn_x, L0_attn_A_log, L0_attn_W_dt, L0_attn_b_dt,
           L0_attn_W_xp, L0_attn_b_xp, L0_attn_W_out, L0_attn_b_out,
           L0_ffn_clip_x, L0_ffn_clip_down, L0_ffn_w_gate, L0_ffn_w_up,
           L0_ffn_w_down):
    x = np.ascontiguousarray(np.asarray(L0_attn_x, dtype=np.float32))
    A_log = np.asarray(L0_attn_A_log, dtype=np.float32)
    W_dt = np.asarray(L0_attn_W_dt, dtype=np.float32)
    b_dt = np.asarray(L0_attn_b_dt, dtype=np.float32)
    W_xp = np.ascontiguousarray(np.asarray(L0_attn_W_xp, dtype=np.float32))
    b_xp = np.asarray(L0_attn_b_xp, dtype=np.float32)
    W_out = np.ascontiguousarray(np.asarray(L0_attn_W_out, dtype=np.float32))
    b_out = np.asarray(L0_attn_b_out, dtype=np.float32)
    clip_x = np.asarray(L0_ffn_clip_x, dtype=np.float32)
    clip_down = np.asarray(L0_ffn_clip_down, dtype=np.float32)
    w_gate = np.ascontiguousarray(np.asarray(L0_ffn_w_gate, dtype=np.float32))
    w_up = np.ascontiguousarray(np.asarray(L0_ffn_w_up, dtype=np.float32))
    w_down = np.ascontiguousarray(np.asarray(L0_ffn_w_down, dtype=np.float32))

    A_img = np.ascontiguousarray(
        A_log.reshape(DT, 128, NS).transpose(1, 0, 2).reshape(128, DT * NS))
    wdt_img = np.ascontiguousarray(W_dt[:, 0].reshape(DT, 128).T)
    bdt_img = np.ascontiguousarray(b_dt.reshape(DT, 128).T)
    bout_img = np.ascontiguousarray(b_out.reshape(DT, 128).T)
    bxp_img = np.ascontiguousarray(b_xp.reshape(32, 1))
    clips_img = np.ascontiguousarray(
        np.array([[clip_x[0], clip_down[0]]], dtype=np.float32))

    nc = build_nc()
    shared = dict(A_img=A_img, wdt_img=wdt_img, bdt_img=bdt_img,
                  wxp=W_xp, bxp_img=bxp_img, wout=W_out, bout_img=bout_img,
                  clips_img=clips_img, w_gate=w_gate, w_up=w_up,
                  w_down=w_down)
    in_maps = []
    for c in range(NCORE):
        m = dict(shared)
        m["x"] = np.ascontiguousarray(
            x[BLOC * c:BLOC * (c + 1)].reshape(BLOC * L, D))
        in_maps.append(m)
    res = run_bass_kernel_spmd(nc, in_maps, core_ids=list(range(NCORE)))
    out = np.empty((16, L, D), dtype=np.float32)
    for c in range(NCORE):
        out[BLOC * c:BLOC * (c + 1)] = np.asarray(
            res.results[c]["out"]).reshape(BLOC, L, D)
    return out


# version 23
# speedup vs baseline: 1.0087x; 1.0087x over previous
import contextlib
import numpy as np
import ml_dtypes
import concourse.bass as bass
import concourse.mybir as mybir
from concourse.bass_utils import run_bass_kernel_spmd
from concourse.bass_isa import ReduceOp

F32 = mybir.dt.float32
BF16 = mybir.dt.bfloat16
I32 = mybir.dt.int32
AF = mybir.ActivationFunctionType
OP = mybir.AluOpType
AX = mybir.AxisListType

D = 1024
L = 2048
BLOC = 2
NT = 16
DT = 8
NS = 16
II = 2816
IT = 22
CH = 4
NCORE = 8
AW = 16384

ENGS = ("sync", "vector", "scalar", "tensor", "gpsimd")


class Buf:
    __slots__ = ("name", "w", "r")

    def __init__(self, name):
        self.name = name
        self.w = None
        self.r = []


NDL = 12
NDG = 8
GN = 0
GASSIGN = [((i + 1) * GN) // NS - (i * GN) // NS == 1 for i in range(16)]


class Plan:
    def __init__(self):
        self.stream = {e: [] for e in ENGS}
        self.cnt = dict(v=0, s=0, t=0, g=0, c=0)
        for i in range(NDL):
            self.cnt[f"dl{i}"] = 0
        for i in range(NDG):
            self.cnt[f"dg{i}"] = 0
        self.chain = dict(sync="c", vector="v", scalar="s", tensor="t",
                          gpsimd="g")
        self.maxwait = {e: {} for e in ENGS}
        self.pending = {e: [] for e in ENGS}
        self.rr = dict(dl=0, dg=0)

    def _sem_of(self, eng, dma):
        if dma:
            q = "dl" if eng in ("sync", "scalar") else "dg"
            n = NDL if q == "dl" else NDG
            i = self.rr[q]
            self.rr[q] = (i + 1) % n
            return f"{q}{i}"
        return self.chain[eng]

    def fence(self):
        snap = [(s, c) for s, c in self.cnt.items() if c > 0]
        for e in ENGS:
            self.pending[e] = list(snap)

    def emit(self, eng, fn, reads=(), writes=(), dma=False, extra=(),
             wait_reads=None, rec_reads=None):
        waits = {}

        def add(s, c):
            waits[s] = max(waits.get(s, 0), c)

        for b in (reads if wait_reads is None else wait_reads):
            if b.w is not None:
                add(*b.w)
        for b in writes:
            if b.w is not None:
                add(*b.w)
            for tok in b.r:
                add(*tok)
        for tok in extra:
            add(*tok)
        for tok in self.pending[eng]:
            add(*tok)
        self.pending[eng] = []
        mw = self.maxwait[eng]
        wl = []
        for s, c in waits.items():
            if mw.get(s, -1) < c:
                wl.append((s, c))
                mw[s] = c
        sem = self._sem_of(eng, dma)
        amt = 16 if dma else 1
        if dma and self.cnt[sem] > 0 and mw.get(sem, -1) < self.cnt[sem]:
            wl.append((sem, self.cnt[sem]))
            mw[sem] = self.cnt[sem]
        self.cnt[sem] += amt
        tok = (sem, self.cnt[sem])
        self.stream[eng].append((tuple(wl), fn, sem, amt))
        for b in writes:
            b.w = tok
            b.r = []
        for b in (reads if rec_reads is None else rec_reads):
            b.r.append(tok)
        return tok


def build_nc(G_QUANT=True, DEBUG=False):
    nc = bass.Bass("TRN2", target_bir_lowering=False)
    p = Plan()
    em = p.emit

    x_d = nc.dram_tensor("x", [BLOC * L, D], F32, kind="ExternalInput").ap()
    A_d = nc.dram_tensor("A_img", [128, DT * NS], F32, kind="ExternalInput").ap()
    wdt_d = nc.dram_tensor("wdt_img", [128, DT], F32, kind="ExternalInput").ap()
    bdt_d = nc.dram_tensor("bdt_img", [128, DT], F32, kind="ExternalInput").ap()
    wxp_d = nc.dram_tensor("wxp", [32, D], F32, kind="ExternalInput").ap()
    bxp_d = nc.dram_tensor("bxp_img", [32, 1], F32, kind="ExternalInput").ap()
    wout_d = nc.dram_tensor("wout", [D, D], F32, kind="ExternalInput").ap()
    bout_d = nc.dram_tensor("bout_img", [128, DT], F32, kind="ExternalInput").ap()
    clip_d = nc.dram_tensor("clips_img", [1, 2], F32, kind="ExternalInput").ap()
    wg_d = nc.dram_tensor("w_gate", [II, D], F32, kind="ExternalInput").ap()
    wu_d = nc.dram_tensor("w_up", [II, D], F32, kind="ExternalInput").ap()
    wd_d = nc.dram_tensor("w_down", [D, II], F32, kind="ExternalInput").ap()
    out_d = nc.dram_tensor("out", [BLOC * L, D], F32, kind="ExternalOutput").ap()

    wgT_d = nc.dram_tensor("wgT_sp", [128, IT * 1024], BF16).ap()
    wuT_d = nc.dram_tensor("wuT_sp", [128, IT * 1024], BF16).ap()
    wdT_d = nc.dram_tensor("wdT_sp", [128, DT * 2816], BF16).ap()
    woT_d = nc.dram_tensor("woT_sp", [DT * 128, D], BF16).ap()
    x1_d = nc.dram_tensor("x1_sp", [BLOC * L, D], F32).ap()
    xl2_d = nc.dram_tensor("xl2_sp", [BLOC * L, D], F32).ap()
    hraw_d = nc.dram_tensor("hraw_sp", [BLOC * II, L], F32).ap()
    dt_d = nc.dram_tensor("dt_sp", [BLOC * DT * 128, L], F32).ap()
    dtx_d = nc.dram_tensor("dtx_sp", [BLOC * DT * 128, L], F32).ap()
    y_d = nc.dram_tensor("y_sp", [BLOC * DT * 128, L], F32).ap()
    if DEBUG:
        dbg_xln = nc.dram_tensor("dbg_xln", [L, D], F32).ap()
        dbg_bc = nc.dram_tensor("dbg_bc", [32, L], F32).ap()
        dbg_dt = nc.dram_tensor("dbg_dt", [128, L], F32).ap()
        dbg_dtx = nc.dram_tensor("dbg_dtx", [128, L], F32).ap()
        dbg_dA = nc.dram_tensor("dbg_dA", [128, L], F32).ap()
        dbg_stt = nc.dram_tensor("dbg_stt", [128, L], F32).ap()
        dbg_y = nc.dram_tensor("dbg_y", [DT * 128, L], F32).ap()
        dbg_attn = nc.dram_tensor("dbg_attn", [DT * 128, L], F32).ap()
        dbg_dB = nc.dram_tensor("dbg_dB", [128, L], F32).ap()
        dbg_sel = nc.dram_tensor("dbg_sel", [32, 32 * 128], BF16).ap()
        dbg_bchi = nc.dram_tensor("dbg_bchi", [32, L], BF16).ap()
        dbg_bclo = nc.dram_tensor("dbg_bclo", [32, L], BF16).ap()
        dbg_bclof = nc.dram_tensor("dbg_bclof", [32, L], F32).ap()
        dbg_bchif = nc.dram_tensor("dbg_bchif", [32, L], F32).ap()
    cc1_in = nc.dram_tensor("cc1_in", [1, 1], F32)
    cc1_out = nc.dram_tensor("cc1_out", [1, 1], F32)
    cc2_in = nc.dram_tensor("cc2_in", [1, 1], F32)
    cc2_out = nc.dram_tensor("cc2_out", [1, 1], F32)

    st_ = contextlib.ExitStack()
    E = st_.enter_context

    def sb(name, shape, dtype=F32):
        return E(nc.sbuf_tensor(name, shape, dtype))

    identf = sb("identf", [128, 128], F32)
    identb = sb("identb", [128, 128], BF16)
    ones_f = sb("ones_f", [1, 128], F32)
    ones_b = sb("ones_b", [1, 128], BF16)
    A_sb = sb("A_sb", [128, DT * NS], F32)
    wdt_sb = sb("wdt_sb", [128, DT], F32)
    bdt_sb = sb("bdt_sb", [128, DT], F32)
    bout_sb = sb("bout_sb", [128, DT], F32)
    bxp_sb = sb("bxp_sb", [32, 1], F32)
    clip_sb = sb("clip_sb", [1, 2], F32)
    clipb_sb = sb("clipb_sb", [128, 2], F32)
    sig_sb = sb("sig_sb", [128, 2], F32)
    wxpT_sb = sb("wxpT", [128, DT * 32], BF16)
    wgs_sb = sb("wgs", [128, IT], F32)
    wus_sb = sb("wus", [128, IT], F32)
    wds_sb = sb("wds", [128, DT], F32)
    sgA_sb = sb("sgA", [128, IT], F32)
    suA_sb = sb("suA", [128, IT], F32)
    sdA_sb = sb("sdA", [128, DT], F32)
    gmx_sb = sb("gmx", [128, 4], F32)
    scal_sb = sb("scal", [128, 8], F32)
    stat_sb = sb("stat", [128, 12], F32)
    wsc_sb = sb("wsc", [128, 16], F32)
    wdisc_sb = sb("wdisc", [128, 16], F32)
    konst_sb = sb("konst", [128, 4], F32)
    selb_sb = sb("selb_sb", [32, 32 * 128], BF16)

    AR1 = sb("AR1", [128, AW], F32)
    AR2 = sb("AR2", [128, AW], F32)
    AR3 = sb("AR3", [128, AW], F32)
    AR1b = AR1.ap().bitcast(BF16)
    AR2b = AR2.ap().bitcast(BF16)
    AR3b = AR3.ap().bitcast(BF16)
    AR1i = AR1.ap().bitcast(I32)
    AR2i = AR2.ap().bitcast(I32)

    pb0 = E(nc.psum_tensor("pb0", [128, 512], F32))
    pb1 = E(nc.psum_tensor("pb1", [128, 512], F32))
    pc0 = E(nc.psum_tensor("pc0", [128, 512], F32))
    pc1 = E(nc.psum_tensor("pc1", [128, 512], F32))
    psY = E(nc.psum_tensor("psY", [128, 2048], F32))
    PB = [pb0, pb1]
    PC = [pc0, pc1]

    sems = {}
    semnames = ["v", "s", "t", "g", "c"] + [f"dl{i}" for i in range(NDL)] + \
        [f"dg{i}" for i in range(NDG)]
    for s in semnames:
        sems[s] = E(nc.semaphore("sem_" + s))

    B_ = {}

    def buf(name):
        if name not in B_:
            B_[name] = Buf(name)
        return B_[name]

    bPB = [buf("pb0"), buf("pb1")]
    bPC = [buf("pc0"), buf("pc1")]
    bPY = [buf(f"psY{c}") for c in range(CH)]

    def ld(dst_ap, src_ap, wbuf, rbufs=(), eng="sync"):
        return em(eng, lambda e, d=dst_ap, s=src_ap: e.dma_start(out=d, in_=s),
                  reads=rbufs, writes=(wbuf,), dma=True)

    def stdma(dst_ap, src_ap, rbuf, wbuf, eng="gpsimd"):
        return em(eng, lambda e, d=dst_ap, s=src_ap: e.dma_start(out=d, in_=s),
                  reads=(rbuf,), writes=(wbuf,), dma=True)

    bid = buf("identf")
    em("gpsimd", lambda g: g.memset(identf[:, :], 0.0), writes=(bid,))
    em("gpsimd", lambda g: g.affine_select(
        out=identf[:, :], in_=identf[:, :], compare_op=OP.not_equal,
        fill=1.0, base=0, pattern=[[-1, 128]], channel_multiplier=1),
        reads=(bid,), writes=(bid,))
    bidb = buf("identb")
    em("gpsimd", lambda g: g.tensor_copy(identb[:, :], identf[:, :]),
       reads=(bid,), writes=(bidb,))
    bonef = buf("ones_f")
    em("gpsimd", lambda g: g.memset(ones_f[:, :], 1.0), writes=(bonef,))
    boneb = buf("ones_b")
    em("gpsimd", lambda g: g.memset(ones_b[:, :], 1.0), writes=(boneb,))
    bkon = buf("konst")
    em("gpsimd", lambda g: g.memset(konst_sb[:, 0:1], 1e-5), writes=(bkon,))
    em("gpsimd", lambda g: g.memset(konst_sb[:, 1:2], -1.0), writes=(bkon,))
    em("gpsimd", lambda g: g.memset(konst_sb[:, 2:3], -0.5), writes=(bkon,))
    em("gpsimd", lambda g: g.memset(konst_sb[:, 3:4], 1.0), writes=(bkon,))
    bsel = buf("sel")
    selftmp = AR1.ap()[0:32, 4096:8192]
    em("gpsimd", lambda g: g.memset(selftmp, 0.0), writes=(bsel,))
    em("gpsimd", lambda g: g.affine_select(
        out=selftmp.rearrange("k (n q) -> k n q", q=128),
        in_=selftmp.rearrange("k (n q) -> k n q", q=128),
        compare_op=OP.not_equal, fill=1.0, base=0,
        pattern=[[-1, 32], [0, 128]], channel_multiplier=1),
        reads=(bsel,), writes=(bsel,))
    em("gpsimd", lambda g: g.tensor_copy(selb_sb[:, :], selftmp),
       reads=(bsel,), writes=(bsel,))
    if DEBUG:
        stdma(dbg_sel, selb_sb[:, :], bsel, buf("dbg"))

    bA = buf("A_sb")
    ld(A_sb[:, :], A_d, bA)
    em("scalar", lambda s: s.activation(A_sb[:, :], A_sb[:, :], AF.Exp),
       reads=(bA,), writes=(bA,))
    em("vector", lambda v: v.tensor_scalar(A_sb[:, :], A_sb[:, :], -1.0, None,
                                           OP.mult), reads=(bA,), writes=(bA,))
    bwdt = buf("wdt"); ld(wdt_sb[:, :], wdt_d, bwdt)
    bbdt = buf("bdt"); ld(bdt_sb[:, :], bdt_d, bbdt)
    bbout = buf("bout"); ld(bout_sb[:, :], bout_d, bbout)
    bbxp = buf("bxp"); ld(bxp_sb[:, :], bxp_d, bbxp)
    bclip = buf("clip"); ld(clip_sb[:, :], clip_d, bclip)
    bclipb = buf("clipb")
    em("tensor", lambda t: t.matmul(pb0[:, 0:2], ones_f[:, :], clip_sb[:, :],
                                    start=True, stop=True),
       reads=(bonef, bclip), writes=(bPB[0],))
    em("scalar", lambda s: s.activation(clipb_sb[:, :], pb0[:, 0:2],
                                        AF.Identity),
       reads=(bPB[0],), writes=(bclipb,))
    bsig = buf("sig")
    em("scalar", lambda s: s.activation(sig_sb[:, :], clipb_sb[:, :], AF.Exp,
                                        scale=konst_sb[:, 1:2]),
       reads=(bclipb, bkon), writes=(bsig,))
    em("vector", lambda v: v.tensor_scalar(sig_sb[:, :], sig_sb[:, :], 1.0,
                                           None, OP.add),
       reads=(bsig,), writes=(bsig,))
    em("vector", lambda v: v.reciprocal(sig_sb[:, :], sig_sb[:, :]),
       reads=(bsig,), writes=(bsig,))
    bgmx = buf("gmx")
    em("vector", lambda v: v.memset(gmx_sb[:, :], 0.0), writes=(bgmx,))

    bwxpf = buf("wxpf")
    wxpf = AR1.ap()[0:32, 0:D]
    ld(wxpf, wxp_d, bwxpf)
    bwxpT = buf("wxpT")
    for j in range(DT):
        pj = PB[j % 2]
        bj = bPB[j % 2]
        em("tensor", lambda t, j=j, pj=pj: t.transpose(
            pj[0:128, 0:32], wxpf[:, 128 * j:128 * (j + 1)],
            identf[0:32, 0:32]), reads=(bwxpf, bid), writes=(bj,))
        em("scalar", lambda s, j=j, pj=pj: s.activation(
            wxpT_sb[:, 32 * j:32 * (j + 1)], pj[0:128, 0:32], AF.Identity),
            reads=(bj,), writes=(bwxpT,))

    bwtmp = buf("wtmp")
    bwoas = buf("wo_as")
    wtmp = AR2.ap()[:, 0:D]
    for r in range(DT):
        ld(wtmp, wout_d[128 * r:128 * (r + 1), :], bwtmp)
        for j in range(DT):
            pj = PB[j % 2]
            bj = bPB[j % 2]
            em("tensor", lambda t, j=j, pj=pj: t.transpose(
                pj[:, 0:128], wtmp[:, 128 * j:128 * (j + 1)], identf[:, :]),
                reads=(bwtmp, bid), writes=(bj,))
            em("scalar", lambda s, r=r, j=j, pj=pj: s.activation(
                AR3b[:, 22528 + 1024 * j + 128 * r:
                     22528 + 1024 * j + 128 * (r + 1)],
                pj[:, 0:128], AF.Identity), reads=(bj,), writes=(bwoas,))
    for j in range(DT):
        stdma(woT_d[128 * j:128 * (j + 1), :],
              AR3b[:, 22528 + 1024 * j:22528 + 1024 * (j + 1)], bwoas,
              buf("woT_spill"))

    p.fence()
    bwsc6 = [buf(f"wsc{k}") for k in range(6)]
    bwsc = buf("wsc_wd")
    bwdisc = buf("wdisc")

    def mk_sets(regions, width):
        out = []
        for k, (AR_, ARi_, o) in enumerate(regions):
            bin_ = buf(f"ch_in_{id(regions)}_{k}")
            s = dict(inp=AR_.ap()[:, o:o + width],
                     a=AR_.ap()[:, o + width:o + 2 * width],
                     sg=AR_.ap()[:, o + 2 * width:o + 3 * width],
                     u0=AR_.ap()[:, o + 3 * width:o + 4 * width],
                     i32=ARi_[:, o:o + width],
                     ng=AR_.ap()[:, o + 4 * width:o + 5 * width])
            bs = dict(a=buf("ch_a"), sg=buf("ch_sg"), u0=buf("ch_u0"),
                      i32=bin_, ng=buf("ch_ng"))
            out.append((s, bin_, bs))
        return out

    def chain_pipe(sets, units, conv_eng, rf2_eng, after_tail=None,
                   front_off=1):
        NU = len(units)

        def pre(u):
            scr, bin_, bscr = sets[u % len(sets)]
            un = units[u]
            ld(scr["inp"][:, 0:un["n"]], un["src"], bin_,
               rbufs=un.get("ldr", ()))
            if "pre" in un:
                un["pre"](scr, bin_)

        def front(u):
            scr, bin_, bscr = sets[u % len(sets)]
            un = units[u]
            N = un["n"]
            isc_ap, bisc = un["isc"]
            em("scalar", lambda s, scr=scr, N=N, isc_ap=isc_ap: s.activation(
                scr["a"][:, 0:N], scr["inp"][:, 0:N], AF.Abs, scale=isc_ap),
               reads=(bin_, bisc), writes=(bscr["a"],))
            em("scalar", lambda s, scr=scr, N=N: s.activation(
                scr["sg"][:, 0:N], scr["inp"][:, 0:N], AF.Sign),
               reads=(bin_,), writes=(bscr["sg"],))

        def mid(u):
            scr, bin_, bscr = sets[u % len(sets)]
            N = units[u]["n"]
            em("vector", lambda v, scr=scr, N=N: v.tensor_scalar(
                scr["u0"][:, 0:N], scr["a"][:, 0:N], 100000.0, 0.5, OP.min,
                OP.add), reads=(bscr["a"],), writes=(bscr["u0"],))
            if conv_eng == "vector":
                em("vector", lambda v, scr=scr, N=N: v.tensor_copy(
                    scr["i32"][:, 0:N], scr["u0"][:, 0:N]),
                   reads=(bscr["u0"],), writes=(bscr["i32"],))
                em("vector", lambda v, scr=scr, N=N: v.tensor_copy(
                    scr["a"][:, 0:N], scr["i32"][:, 0:N]),
                   reads=(bscr["i32"],), writes=(bscr["a"],))
            else:
                em("scalar", lambda s, scr=scr, N=N: s.copy(
                    scr["i32"][:, 0:N], scr["u0"][:, 0:N]),
                   reads=(bscr["u0"],), writes=(bscr["i32"],))
                em("scalar", lambda s, scr=scr, N=N: s.copy(
                    scr["a"][:, 0:N], scr["i32"][:, 0:N]),
                   reads=(bscr["i32"],), writes=(bscr["a"],))
            em("vector", lambda v, scr=scr, N=N: v.tensor_tensor(
                scr["ng"][:, 0:N], scr["u0"][:, 0:N], scr["a"][:, 0:N],
                OP.is_lt), reads=(bscr["u0"], bscr["a"]),
                writes=(bscr["ng"],))
            em("vector", lambda v, scr=scr, N=N: v.scalar_tensor_tensor(
                scr["u0"][:, 0:N], scr["a"][:, 0:N], 1.0, scr["ng"][:, 0:N],
                OP.mult, OP.subtract),
               reads=(bscr["a"], bscr["ng"]), writes=(bscr["u0"],))

        def tail(u):
            scr, bin_, bscr = sets[u % len(sets)]
            un = units[u]
            N = un["n"]
            em(rf2_eng, lambda e, scr=scr, N=N: e.tensor_tensor(
                scr["ng"][:, 0:N], scr["u0"][:, 0:N], scr["sg"][:, 0:N],
                OP.mult), reads=(bscr["u0"], bscr["sg"]),
                writes=(bscr["ng"],))
            out_ap, bout_ = un["out"](scr, bscr)
            em("vector", lambda v, scr=scr, N=N, out_ap=out_ap:
               v.tensor_scalar(out_ap, scr["ng"][:, 0:N], 127.0, -128.0,
                               OP.min, OP.max),
               reads=(bscr["ng"],), writes=(bout_,))
            if "tail" in un:
                un["tail"](scr, bin_, bscr)

        def tail2(u):
            scr, bin_, bscr = sets[u % len(sets)]
            un = units[u]
            if "tail2" in un:
                un["tail2"](scr, bin_, bscr)

        for i in range(NU + 4):
            if i < NU:
                pre(i)
                if front_off == 0:
                    front(i)
            if front_off == 1 and 1 <= i and i - 1 < NU:
                front(i - 1)
            if 2 <= i and i - 2 < NU:
                mid(i - 2)
            if 3 <= i and i - 3 < NU:
                tail(i - 3)
                if after_tail is not None:
                    after_tail(i - 3)
            if 4 <= i:
                tail2(i - 4)

    wq3 = mk_sets([(AR1, AR1i, 0), (AR1, AR1i, 5120), (AR1, AR1i, 10240),
                   (AR2, AR2i, 0), (AR2, AR2i, 5120), (AR2, AR2i, 10240)],
                  1024)
    basm = buf("wq_asm")
    bwgs = buf("wgs"); bwus = buf("wus"); bwds = buf("wds")

    def wq_pre(scl_sb, scl_buf, tt, iscc, bws):
        def _pre(scr, bin_, tt=tt, iscc=iscc, bws=bws):
            em("vector", lambda v, scr=scr: v.tensor_reduce(
                wsc_sb[:, iscc:iscc + 1], scr["inp"][:, 0:1024], AX.X,
                OP.max, apply_absolute_value=True),
               reads=(bin_,), writes=(bws,))
            em("vector", lambda v: v.tensor_scalar(
                scl_sb[:, tt:tt + 1], wsc_sb[:, iscc:iscc + 1],
                float(np.float32(1.0 / 127.0)), None, OP.mult),
                reads=(bws,), writes=(scl_buf,))
            em("vector", lambda v: v.reciprocal(
                wsc_sb[:, iscc + 1:iscc + 2], wsc_sb[:, iscc:iscc + 1]),
               reads=(bws,), writes=(bws,))
            em("vector", lambda v: v.tensor_scalar(
                wsc_sb[:, iscc + 1:iscc + 2], wsc_sb[:, iscc + 1:iscc + 2],
                127.0, None, OP.mult),
                reads=(bws,), writes=(bws,))
        return _pre

    def wq_tail(u, nblk):
        def _tail(scr, bin_, bscr, u=u, nblk=nblk):
            PJ = PB if u % 2 == 0 else PC
            bPJ = bPB if u % 2 == 0 else bPC
            for ll in range(nblk):
                g = ll // 4
                em("tensor", lambda t, ll=ll, pj=PJ[g], scr=scr: t.transpose(
                    pj[:, 128 * (ll % 4):128 * (ll % 4) + 128],
                    scr["sg"][:, 128 * ll:128 * (ll + 1)],
                    identf[:, :]), reads=(bscr["sg"], bid), writes=(bPJ[g],))
        return _tail

    def wq_evac(u, nblk, asm_col):
        def _ev(scr, bin_, bscr, u=u, nblk=nblk, asm_col=asm_col):
            PJ = PB if u % 2 == 0 else PC
            bPJ = bPB if u % 2 == 0 else bPC
            for g in range((nblk + 3) // 4):
                w = 128 * min(4, nblk - 4 * g)
                em("scalar", lambda s, g=g, w=w, pj=PJ[g], asm_col=asm_col:
                   s.activation(
                       AR3b[:, asm_col + 512 * g:asm_col + 512 * g + w],
                       pj[:, 0:w], AF.Identity), reads=(bPJ[g],),
                   writes=(basm,))
        return _ev

    def wq_out(N):
        return lambda scr, bscr: (scr["sg"][:, 0:N], bscr["sg"])

    for src_d, scl_sb, scl_buf, dstT_d in (
            (wg_d, wgs_sb, bwgs, wgT_d), (wu_d, wus_sb, bwus, wuT_d)):
        units = []
        for tt in range(IT):
            sp = tt % 6
            units.append(dict(
                src=src_d[128 * tt:128 * (tt + 1), :], n=1024,
                isc=(wsc_sb[:, 2 * sp + 1:2 * sp + 2], bwsc6[sp]),
                pre=wq_pre(scl_sb, scl_buf, tt, 2 * sp, bwsc6[sp]),
                out=wq_out(1024),
                tail=wq_tail(tt, DT), tail2=wq_evac(tt, DT, 1024 * tt)))
        chain_pipe(wq3, units, "vector", "vector", front_off=0)
        stdma(dstT_d[:, :], AR3b[:, 0:IT * 1024], basm, buf("wqT_spill"))

    for tt in range(DT):
        for cc in range(3):
            w = 1024 if cc < 2 else 768
            scr, bin_, bscr = wq3[(3 * tt + cc) % len(wq3)]
            ld(scr["inp"][:, 0:w],
               wd_d[128 * tt:128 * (tt + 1), 1024 * cc:1024 * cc + w], bin_)
            if cc == 0:
                em("vector", lambda v, scr=scr, w=w, tt=tt: v.tensor_reduce(
                    wdisc_sb[:, 2 * tt:2 * tt + 1], scr["inp"][:, 0:w], AX.X,
                    OP.max, apply_absolute_value=True),
                   reads=(bin_,), writes=(bwdisc,))
            else:
                tc = 12
                em("vector", lambda v, scr=scr, w=w, tc=tc: v.tensor_reduce(
                    wsc_sb[:, tc:tc + 1], scr["inp"][:, 0:w], AX.X,
                    OP.max, apply_absolute_value=True),
                   reads=(bin_,), writes=(bwsc,))
                em("vector", lambda v, tt=tt, tc=tc: v.tensor_tensor(
                    wdisc_sb[:, 2 * tt:2 * tt + 1],
                    wdisc_sb[:, 2 * tt:2 * tt + 1], wsc_sb[:, tc:tc + 1],
                    OP.max), reads=(bwdisc, bwsc), writes=(bwdisc,))
    for tt in range(DT):
        em("vector", lambda v, tt=tt: v.tensor_scalar(
            wds_sb[:, tt:tt + 1], wdisc_sb[:, 2 * tt:2 * tt + 1],
            float(np.float32(1.0 / 127.0)), None, OP.mult),
            reads=(bwdisc,), writes=(bwds,))
        em("vector", lambda v, tt=tt: v.reciprocal(
            wdisc_sb[:, 2 * tt + 1:2 * tt + 2],
            wdisc_sb[:, 2 * tt:2 * tt + 1]),
           reads=(bwdisc,), writes=(bwdisc,))
        em("vector", lambda v, tt=tt: v.tensor_scalar(
            wdisc_sb[:, 2 * tt + 1:2 * tt + 2],
            wdisc_sb[:, 2 * tt + 1:2 * tt + 2], 127.0, None, OP.mult),
            reads=(bwdisc,), writes=(bwdisc,))
    units = []
    for u0_, (tt, cc) in enumerate([(tt, cc) for tt in range(DT)
                                    for cc in range(3)]):
        w = 1024 if cc < 2 else 768
        units.append(dict(
            src=wd_d[128 * tt:128 * (tt + 1), 1024 * cc:1024 * cc + w],
            n=w, isc=(wdisc_sb[:, 2 * tt + 1:2 * tt + 2], bwdisc),
            out=wq_out(w),
            tail=wq_tail(u0_, w // 128),
            tail2=wq_evac(u0_, w // 128, 2816 * tt + 128 * 8 * cc)))
    chain_pipe(wq3, units, "vector", "vector", front_off=0)
    stdma(wdT_d[:, :], AR3b[:, 0:DT * 2816], basm, buf("wqT_spill"))

    p.fence()

    xlnT = AR1.ap()
    xfm = AR2.ap()
    xbf = AR3b
    sbc = AR3.ap()[:, 14336:16384]
    dtv = [AR1.ap()[:, 0:2048], AR1.ap()[:, 2048:4096]]
    dtxv = [AR1.ap()[:, 4096:6144], AR1.ap()[:, 6144:8192]]
    bcv = AR3.ap()[0:32, 8192:10240]
    bchiv = AR3b[0:32, 20480:22528]
    bclov = AR3b[0:32, 22528:24576]
    bclofv = AR3.ap()[0:32, 12288:14336]
    cacheB = [AR1.ap()[:, 2048 * k:2048 * (k + 1)] for k in range(4)]
    cacheC = [AR1.ap()[:, 8192 + 2048 * k:8192 + 2048 * (k + 1)]
              for k in range(4)]
    dts = [AR3.ap()[:, 0:2048], AR3.ap()[:, 2048:4096]]
    dtxs = [AR3.ap()[:, 4096:6144], AR3.ap()[:, 6144:8192]]
    dAv = [AR3.ap()[:, 12288:14336], AR3.ap()[:, 14336:16384]]
    sttv = [AR2.ap()[:, 0:2048], AR2.ap()[:, 2048:4096]]
    dBv = [AR2.ap()[:, 4096:6144], AR2.ap()[:, 6144:8192]]
    prodv = [AR2b[:, 16384:18432], AR2b[:, 18432:20480]]
    ybufv = [AR2.ap()[:, 10240:12288], AR2.ap()[:, 12288:14336]]
    ybf = AR1b
    attn = AR2.ap()
    xTv2 = [AR3.ap()[:, 0:1024], AR3.ap()[:, 1024:2048]]
    xT2v2 = [AR3.ap()[:, 10240:11264], AR3.ap()[:, 11264:12288]]
    x1Tv = [AR3.ap()[:, 2048:3072], AR3.ap()[:, 3072:4096]]
    xl2Tv = [AR3.ap()[:, 4096:5120], AR3.ap()[:, 5120:6144]]
    woTl = AR3b[:, 12288:20480]

    bxlnT = [buf(f"xlnT{t}") for t in range(NT)]
    bxfm = [buf(f"xfm{j}") for j in range(DT)]
    bxbf = [buf(f"xbf{j}") for j in range(DT)]
    bstat = buf("stat")
    bsbc = buf("sbc")
    bdt = [buf("dt0"), buf("dt1")]
    bdtx = [buf("dtx0"), buf("dtx1")]
    bdts = [buf("dts0"), buf("dts1")]
    bdtxs = [buf("dtxs0"), buf("dtxs1")]
    bdA = [buf("dA0"), buf("dA1")]
    bdB = [buf("dB0"), buf("dB1")]
    bstt = [buf("stt0"), buf("stt1")]
    bcB = [buf(f"cacheB{k}") for k in range(4)]
    bcC = [buf(f"cacheC{k}") for k in range(4)]
    bybuf = [buf("ybuf0"), buf("ybuf1")]
    bdtd = [buf(f"dtd{b}_{j}") for b in range(BLOC) for j in range(DT)]
    bdtxd = [buf(f"dtxd{b}_{j}") for b in range(BLOC) for j in range(DT)]
    byd = [buf(f"yd{b}_{j}") for b in range(BLOC) for j in range(DT)]
    bbc = buf("bc"); bbchi = buf("bchi"); bbclo = buf("bclo")
    bbclof = buf("bclof")
    bprod = [buf("prod0"), buf("prod1")]
    bybf = [buf(f"ybf{j}") for j in range(DT)]
    battn = [buf(f"attn{m}") for m in range(DT)]
    bxTp = [buf("xT0"), buf("xT1")]
    bxT2p = [buf("xT20"), buf("xT21")]
    bstatL = [buf("statL0"), buf("statL1")]
    bx1T = [buf("x1T0"), buf("x1T1")]
    bxl2T = [buf("xl2T0"), buf("xl2T1")]
    bwoTl = buf("woTl")

    def ln_tile(src_buf_ap, bsrc, dst_ap, bdst, pp=0):
        o = 8 * pp
        bst = bstatL[pp]
        xT2v = xT2v2[pp]
        bxT2 = bxT2p[pp]
        em("vector", lambda v: v.reduce_sum(stat_sb[:, o:o + 1], src_buf_ap,
                                            AX.X),
           reads=(bsrc,), writes=(bst,))
        em("scalar", lambda s: s.activation(xT2v, src_buf_ap, AF.Square),
           reads=(bsrc,), writes=(bxT2,))
        em("vector", lambda v: v.reduce_sum(stat_sb[:, o + 1:o + 2], xT2v,
                                            AX.X),
           reads=(bxT2,), writes=(bst,))
        em("vector", lambda v: v.tensor_scalar(
            stat_sb[:, o:o + 2], stat_sb[:, o:o + 2],
            float(np.float32(1.0 / D)), None,
            OP.mult), reads=(bst,), writes=(bst,))
        em("vector", lambda v: v.tensor_tensor(
            stat_sb[:, o + 2:o + 3], stat_sb[:, o:o + 1],
            stat_sb[:, o:o + 1], OP.mult),
            reads=(bst,), writes=(bst,))
        em("vector", lambda v: v.tensor_tensor(
            stat_sb[:, o + 2:o + 3], stat_sb[:, o + 1:o + 2],
            stat_sb[:, o + 2:o + 3], OP.subtract),
            reads=(bst,), writes=(bst,))
        em("scalar", lambda s: s.activation(
            stat_sb[:, o + 3:o + 4], stat_sb[:, o + 2:o + 3], AF.Ln,
            bias=konst_sb[:, 0:1]),
            reads=(bst, bkon), writes=(bst,))
        em("scalar", lambda s: s.activation(
            stat_sb[:, o + 3:o + 4], stat_sb[:, o + 3:o + 4], AF.Exp,
            scale=konst_sb[:, 2:3]),
            reads=(bst, bkon), writes=(bst,))
        em("vector", lambda v: v.tensor_scalar(
            dst_ap, src_buf_ap, stat_sb[:, o:o + 1], stat_sb[:, o + 3:o + 4],
            OP.subtract, OP.mult), reads=(bsrc, bst), writes=(bdst,))

    for b in range(BLOC):
        p.fence()
        for tt in range(NT):
            row0 = b * L + 128 * tt
            tp = tt % 2
            ld(xTv2[tp], x_d[row0:row0 + 128, :], bxTp[tp])
            ln_tile(xTv2[tp], bxTp[tp], xlnT[:, D * tt:D * (tt + 1)],
                    bxlnT[tt], pp=tp)
        if DEBUG and b == 0:
            for tt in range(NT):
                stdma(dbg_xln[128 * tt:128 * (tt + 1), :],
                      xlnT[:, D * tt:D * (tt + 1)], bxlnT[tt], buf("dbg"))
        p.fence()
        PJ4 = [pb0, pb1, pc0, pc1]
        bPJ4 = [bPB[0], bPB[1], bPC[0], bPC[1]]
        for j in range(DT):
            for tt in range(NT):
                pj = PJ4[tt % 4]
                bj = bPJ4[tt % 4]
                em("tensor", lambda t, j=j, tt=tt, pj=pj: t.transpose(
                    pj[:, 0:128],
                    xlnT[:, D * tt + 128 * j:D * tt + 128 * (j + 1)],
                    identf[:, :]), reads=(bxlnT[tt], bid), writes=(bj,))
                em("scalar", lambda s, j=j, tt=tt, pj=pj: s.activation(
                    xfm[:, 2048 * j + 128 * tt:2048 * j + 128 * (tt + 1)],
                    pj[:, 0:128], AF.Identity), reads=(bj,), writes=(bxfm[j],))
            em("scalar", lambda s, j=j: s.activation(
                xbf[:, 2048 * j:2048 * (j + 1)],
                xfm[:, 2048 * j:2048 * (j + 1)], AF.Identity),
                reads=(bxfm[j],), writes=(bxbf[j],))
        for c in range(CH):
            pj = PB[c % 2]
            bj = bPB[c % 2]
            for j in range(DT):
                em("tensor", lambda t, c=c, j=j, pj=pj: t.matmul(
                    pj[0:32, :], wxpT_sb[:, 32 * j:32 * (j + 1)],
                    xbf[:, 2048 * j + 512 * c:2048 * j + 512 * (c + 1)],
                    start=(j == 0), stop=(j == DT - 1)),
                    reads=(bwxpT, bxbf[j]), writes=(bj,))
            em("scalar", lambda s, c=c, pj=pj: s.activation(
                bcv[:, 512 * c:512 * (c + 1)], pj[0:32, :], AF.Identity,
                bias=bxp_sb[:, 0:1]), reads=(bj, bbxp), writes=(bbc,))
        em("vector", lambda v: v.tensor_copy(bchiv, bcv), reads=(bbc,),
           writes=(bbchi,))
        em("vector", lambda v: v.tensor_copy(bclofv, bchiv), reads=(bbchi,),
           writes=(bbclof,))
        if DEBUG and b == 0:
            stdma(dbg_bchif, bclofv, bbclof, buf("dbg"))
        em("vector", lambda v: v.tensor_tensor(bclofv, bcv, bclofv,
                                               OP.subtract),
           reads=(bbc, bbclof), writes=(bbclof,))
        if DEBUG and b == 0:
            stdma(dbg_bclof, bclofv, bbclof, buf("dbg"))
        em("vector", lambda v: v.tensor_copy(bclov, bclofv), reads=(bbclof,),
           writes=(bbclo,))
        if DEBUG and b == 0:
            stdma(dbg_bc, bcv, bbc, buf("dbg"))
            stdma(dbg_bchi, bchiv, bbchi, buf("dbg"))
            stdma(dbg_bclo, bclov, bbclo, buf("dbg"))
        p.fence()
        for c in range(CH):
            pj = PC[c % 2]
            bj = bPC[c % 2]
            em("tensor", lambda t, c=c, pj=pj: t.matmul(
                pj[:, :], ones_f[:, :],
                xfm[0:1, 512 * c:512 * (c + 1)], start=True, stop=True),
                reads=(bonef, bxfm[0]), writes=(bj,))
            em("scalar", lambda s, c=c, pj=pj: s.activation(
                sbc[:, 512 * c:512 * (c + 1)], pj[:, :], AF.Identity),
                reads=(bj,), writes=(bsbc,))
        for j in range(DT):
            jp = j % 2
            em("scalar", lambda s, j=j, jp=jp: s.activation(
                dtv[jp], sbc, AF.Exp, scale=wdt_sb[:, j:j + 1],
                bias=bdt_sb[:, j:j + 1]),
                reads=(bsbc, bwdt, bbdt), writes=(bdt[jp],))
            em("scalar", lambda s, jp=jp: s.activation(
                dtv[jp], dtv[jp], AF.Ln, bias=konst_sb[:, 3:4]),
                reads=(bdt[jp], bkon), writes=(bdt[jp],))
            em("vector", lambda v, j=j, jp=jp: v.tensor_tensor(
                dtxv[jp], dtv[jp], xfm[:, 2048 * j:2048 * (j + 1)], OP.mult),
                reads=(bdt[jp], bxfm[j]), writes=(bdtx[jp],))
            if DEBUG and b == 0 and j == 0:
                stdma(dbg_dt, dtv[jp], bdt[jp], buf("dbg"))
                stdma(dbg_dtx, dtxv[jp], bdtx[jp], buf("dbg"))
            r0 = (b * DT + j) * 128
            stdma(dt_d[r0:r0 + 128, :], dtv[jp], bdt[jp], bdtd[b * DT + j])
            stdma(dtx_d[r0:r0 + 128, :], dtxv[jp], bdtx[jp], bdtxd[b * DT + j])
        p.fence()

        def build_cache_nn(nq, nn):
            n = 4 * nq + nn
            for src_row, dst, bdst in ((n, cacheB[nn], bcB[nn]),
                                       (NS + n, cacheC[nn], bcC[nn])):
                for c2 in range(0, CH, 2):
                    for c in (c2, c2 + 1):
                        em("tensor",
                           lambda t, r=src_row, c=c, pj=PB[c % 2]:
                           t.matmul(
                               pj[:, :], selb_sb[:, 128 * r:128 * (r + 1)],
                               bchiv[:, 512 * c:512 * (c + 1)],
                               start=True, stop=False),
                           reads=(bsel, bbchi), writes=(bPB[c % 2],))
                    for c in (c2, c2 + 1):
                        em("tensor",
                           lambda t, r=src_row, c=c, pj=PB[c % 2]:
                           t.matmul(
                               pj[:, :], selb_sb[:, 128 * r:128 * (r + 1)],
                               bclov[:, 512 * c:512 * (c + 1)],
                               start=False, stop=True),
                           reads=(bsel, bbclo), writes=(bPB[c % 2],))
                    for c in (c2, c2 + 1):
                        em("scalar", lambda s, c=c, pj=PB[c % 2], dst=dst:
                           s.activation(dst[:, 512 * c:512 * (c + 1)],
                                        pj[:, :], AF.Identity),
                           reads=(bPB[c % 2],), writes=(bdst,))

        for nn in range(4):
            build_cache_nn(0, nn)
        for nq in range(4):
            for j in range(DT):
                jp = j % 2
                r0 = (b * DT + j) * 128
                ld(dts[jp], dt_d[r0:r0 + 128, :], bdts[jp],
                   rbufs=(bdtd[b * DT + j],))
                ld(dtxs[jp], dtx_d[r0:r0 + 128, :], bdtxs[jp],
                   rbufs=(bdtxd[b * DT + j],))
                def emit_dAdB(nn, j=j, jp=jp, nq=nq):
                    n = 4 * nq + nn
                    npar = nn % 2
                    geng = "gpsimd" if GASSIGN[n] else "vector"
                    em("scalar", lambda s, j=j, n=n, jp=jp, npar=npar:
                       s.activation(dAv[npar], dts[jp], AF.Exp,
                                    scale=A_sb[:, NS * j + n:NS * j + n + 1]),
                       reads=(bdts[jp], bA), writes=(bdA[npar],))
                    em(geng, lambda e, nn=nn, jp=jp, npar=npar: e.tensor_tensor(
                        dBv[npar], dtxs[jp], cacheB[nn], OP.mult),
                        reads=(bdtxs[jp], bcB[nn]), writes=(bdB[npar],))

                emit_dAdB(0)
                emit_dAdB(1)
                for nn in range(4):
                    n = 4 * nq + nn
                    npar = nn % 2
                    geng = "gpsimd" if GASSIGN[n] else "vector"
                    em("vector", lambda v, npar=npar: v.tensor_tensor_scan(
                        sttv[npar], dAv[npar], dBv[npar], 0.0, OP.mult,
                        OP.add),
                        reads=(bdA[npar], bdB[npar]), writes=(bstt[npar],))
                    if DEBUG and b == 0 and j == 0 and n == 0:
                        stdma(dbg_dA, dAv[npar], bdA[npar], buf("dbg"))
                        stdma(dbg_stt, sttv[npar], bstt[npar], buf("dbg"))
                        stdma(dbg_dB, dBv[npar], bdB[npar], buf("dbg"))
                    if nn + 2 < 4:
                        emit_dAdB(nn + 2)
                    em(geng, lambda e, nn=nn, npar=npar: e.tensor_tensor(
                        prodv[npar], sttv[npar], cacheC[nn], OP.mult),
                        reads=(bstt[npar], bcC[nn]), writes=(bprod[npar],))
                    for c in range(CH):
                        em("tensor", lambda t, nn=nn, c=c, npar=npar: t.matmul(
                            psY[:, 512 * c:512 * (c + 1)], identb[:, :],
                            prodv[npar][:, 512 * c:512 * (c + 1)],
                            start=(nn == 0), stop=(nn == 3)),
                            reads=(bidb, bprod[npar]), writes=(bPY[c],))
                    if j == DT - 1 and nq < 3:
                        build_cache_nn(nq + 1, nn)
                em("scalar", lambda s, jp=jp: s.activation(
                    ybufv[jp], psY[:, :], AF.Identity),
                    reads=tuple(bPY), writes=(bybuf[jp],))
                em("gpsimd", lambda g, b=b, j=j, jp=jp, nq=nq: g.dma_start(
                    out=y_d[(b * DT + j) * 128:(b * DT + j) * 128 + 128, :],
                    in_=ybufv[jp],
                    **({} if nq == 0 else dict(accum_op=OP.add))),
                    reads=(bybuf[jp],), writes=(byd[b * DT + j],), dma=True)
        p.fence()
        for j in range(DT):
            r0 = (b * DT + j) * 128
            em("gpsimd", lambda g, j=j, r0=r0: g.dma_start(
                out=ybf[:, 2048 * j:2048 * (j + 1)],
                in_=y_d[r0:r0 + 128, :]),
                reads=(byd[b * DT + j],), writes=(bybf[j],), dma=True)
        for j in range(DT):
            ld(AR3b[:, 12288 + 1024 * j:12288 + 1024 * (j + 1)],
               woT_d[128 * j:128 * (j + 1), :], bwoTl)
        o_in = (bwoTl,) + tuple(bybf)
        ogroups = [(m, c) for m in range(DT) for c in range(CH)]
        for gi in range(0, len(ogroups), 2):
            pair = ogroups[gi:gi + 2]
            PJ = PB if (gi // 2) % 2 == 0 else PC
            bPJ = bPB if (gi // 2) % 2 == 0 else bPC
            for k in range(DT):
                for pi, (m, c) in enumerate(pair):
                    pj = PJ[pi]
                    bj = bPJ[pi]
                    em("tensor", lambda t, m=m, c=c, k=k, pj=pj: t.matmul(
                        pj[:, :],
                        woTl[:, 1024 * k + 128 * m:1024 * k + 128 * (m + 1)],
                        ybf[:, 2048 * k + 512 * c:2048 * k + 512 * (c + 1)],
                        start=(k == 0), stop=(k == DT - 1)),
                        reads=(), writes=(bj,),
                        wait_reads=o_in if gi == 0 and k == 0 and pi == 0
                        else (),
                        rec_reads=o_in if gi == len(ogroups) - 2 and
                        k == DT - 1 and pi == 1 else ())
            for pi, (m, c) in enumerate(pair):
                pj = PJ[pi]
                bj = bPJ[pi]
                em("scalar", lambda s, m=m, c=c, pj=pj: s.activation(
                    attn[:, 2048 * m + 512 * c:2048 * m + 512 * (c + 1)],
                    pj[:, :], AF.Identity, bias=bout_sb[:, m:m + 1]),
                    reads=(bj, bbout), writes=(battn[m],))
        if DEBUG and b == 0:
            for m in range(DT):
                stdma(dbg_attn[128 * m:128 * (m + 1), :],
                      attn[:, 2048 * m:2048 * (m + 1)], battn[m], buf("dbg"))
        for tt in range(NT):
            tp = tt % 2
            row0 = b * L + 128 * tt
            ld(xTv2[tp], x_d[row0:row0 + 128, :], bxTp[tp])
            for m in range(DT):
                pj = PJ4[m % 4]
                bj = bPJ4[m % 4]
                em("tensor", lambda t, tt=tt, m=m, pj=pj: t.transpose(
                    pj[:, 0:128],
                    attn[:, 2048 * m + 128 * tt:2048 * m + 128 * (tt + 1)],
                    identf[:, :]), reads=(battn[m], bid), writes=(bj,))
                em("vector", lambda v, m=m, tp=tp, pj=pj: v.tensor_tensor(
                    x1Tv[tp][:, 128 * m:128 * (m + 1)],
                    xTv2[tp][:, 128 * m:128 * (m + 1)], pj[:, 0:128], OP.add),
                    reads=(bxTp[tp], bj), writes=(bx1T[tp],))
            stdma(x1_d[row0:row0 + 128, :], x1Tv[tp], bx1T[tp],
                  buf("x1_spill"))
            ln_tile(x1Tv[tp], bx1T[tp], xl2Tv[tp], bxl2T[tp], pp=tp)
            em("vector", lambda v, tp=tp: v.tensor_reduce(
                stat_sb[:, 4:5], xl2Tv[tp], AX.X, apply_absolute_value=True,
                op=OP.max), reads=(bxl2T[tp],), writes=(bstat,))
            em("vector", lambda v: v.tensor_tensor(
                gmx_sb[:, 0:1], gmx_sb[:, 0:1], stat_sb[:, 4:5], OP.max),
                reads=(bgmx, bstat), writes=(bgmx,))
            stdma(xl2_d[row0:row0 + 128, :], xl2Tv[tp], bxl2T[tp],
                  buf("xl2_spill"))

    em("gpsimd", lambda g: g.tensor_reduce(
        gmx_sb[0:1, 1:2], gmx_sb[:, 0:1], AX.C, OP.max),
        reads=(bgmx,), writes=(bgmx,))
    bcc1 = buf("cc1")
    stdma(cc1_in.ap(), gmx_sb[0:1, 1:2], bgmx, bcc1)
    em("gpsimd", lambda g: g.collective_compute(
        "AllReduce", OP.max, replica_groups=[list(range(NCORE))],
        ins=[cc1_in.ap().opt()], outs=[cc1_out.ap().opt()]),
        reads=(bcc1,), writes=(bcc1,))
    ld(gmx_sb[0:1, 1:2], cc1_out.ap(), bgmx, rbufs=(bcc1,), eng="gpsimd")
    em("tensor", lambda t: t.matmul(pb0[:, 0:1], ones_f[:, :],
                                    gmx_sb[0:1, 1:2], start=True, stop=True),
       reads=(bonef, bgmx), writes=(bPB[0],))
    em("scalar", lambda s: s.activation(gmx_sb[:, 1:2], pb0[:, 0:1],
                                        AF.Identity),
       reads=(bPB[0],), writes=(bgmx,))
    bscal = buf("scal")
    em("vector", lambda v: v.tensor_tensor(scal_sb[:, 0:1], sig_sb[:, 0:1],
                                           gmx_sb[:, 1:2], OP.mult),
       reads=(bsig, bgmx), writes=(bscal,))
    em("vector", lambda v: v.reciprocal(scal_sb[:, 1:2], scal_sb[:, 0:1]),
       reads=(bscal,), writes=(bscal,))
    em("vector", lambda v: v.tensor_scalar(scal_sb[:, 1:2], scal_sb[:, 1:2],
                                           127.0, None, OP.mult),
       reads=(bscal,), writes=(bscal,))
    em("vector", lambda v: v.tensor_scalar(scal_sb[:, 2:3], scal_sb[:, 0:1],
                                           float(np.float32(1.0 / 127.0)), None,
                                           OP.mult),
       reads=(bscal,), writes=(bscal,))
    bsgA = buf("sgA"); bsuA = buf("suA")
    em("vector", lambda v: v.tensor_scalar(sgA_sb[:, :], wgs_sb[:, :],
                                           scal_sb[:, 2:3], None, OP.mult),
       reads=(bwgs, bscal), writes=(bsgA,))
    em("vector", lambda v: v.tensor_scalar(suA_sb[:, :], wus_sb[:, :],
                                           scal_sb[:, 2:3], None, OP.mult),
       reads=(bwus, bscal), writes=(bsuA,))

    p.fence()
    wgT = AR1b
    wuT = AR2b
    xq3 = mk_sets([(AR1, AR1i, 11264), (AR1, AR1i, 13824),
                   (AR2, AR2i, 13312)], 512)
    sgatev = [AR2.ap()[:, 11264:11776], AR2.ap()[:, 11776:12288]]
    hrawcv = [AR2.ap()[:, 12288:12800], AR2.ap()[:, 12800:13312]]

    bwgT = buf("wgT"); bwuT = buf("wuT")
    bxqfm2 = [[buf(f"xqfm{b}_{j}") for j in range(DT)] for b in range(BLOC)]
    bsgate = [buf("sgate0"), buf("sgate1")]
    bhrawc = [buf("hrawc0"), buf("hrawc1")]
    bhspill = {}

    for j in range(DT):
        off = 512 * (j // 4) + 128 * (j % 4)
        ld(AR1b[:, II * j:II * (j + 1)],
           bass.AP(wgT_d.tensor, off, [[IT * 1024, 128], [1024, IT],
                                       [1, 128]]), bwgT)
        ld(AR2b[:, II * j:II * (j + 1)],
           bass.AP(wuT_d.tensor, off, [[IT * 1024, 128], [1024, IT],
                                       [1, 128]]), bwuT)

    def xq_tail(b, tt, h, up):
        def _tail(scr, bin_, bscr):
            pj = PC[up]
            bj = bPC[up]
            for jj in range(4):
                em("tensor", lambda t, jj=jj, pj=pj, scr=scr: t.transpose(
                    pj[:, 128 * jj:128 * (jj + 1)],
                    scr["sg"][:, 128 * jj:128 * (jj + 1)],
                    identf[:, :]), reads=(bscr["sg"], bid), writes=(bj,))
            for jj in range(4):
                j = 4 * h + jj
                em("scalar", lambda s, b=b, j=j, jj=jj, tt=tt, pj=pj:
                   s.activation(
                       AR3b[:, 16384 * b + 2048 * j + 128 * tt:
                            16384 * b + 2048 * j + 128 * (tt + 1)],
                       pj[:, 128 * jj:128 * (jj + 1)], AF.Identity),
                   reads=(bj,), writes=(bxqfm2[b][j],))
        return _tail

    def xq_units(b, tts):
        us = []
        for tt in tts:
            row0 = b * L + 128 * tt
            for h in range(2):
                us.append(dict(
                    src=xl2_d[row0:row0 + 128, 512 * h:512 * (h + 1)],
                    n=512, isc=(scal_sb[:, 1:2], bscal), out=wq_out(512),
                    tail=xq_tail(b, tt, h, (2 * tt + h) % 2)))
        return us

    chain_pipe(xq3, xq_units(0, range(NT)), "vector", "vector")

    UPB = [pb0, pb1]
    bUPB = [bPB[0], bPB[1]]
    for b in range(BLOC):
        bxqfm = bxqfm2[b]
        xo = 16384 * b
        for it in range(IT):
            for c in range(CH):
                g_in = (bwgT,) + tuple(bxqfm)
                u_in = (bwuT,) + tuple(bxqfm)
                pu = UPB[c % 2]; bu = bUPB[c % 2]
                for k in range(DT):
                    em("tensor", lambda t, it=it, c=c, k=k, xo=xo: t.matmul(
                        psY[:, 512 * c:512 * (c + 1)],
                        AR1b[:, II * k + 128 * it:II * k + 128 * (it + 1)],
                        AR3b[:, xo + 2048 * k + 512 * c:
                             xo + 2048 * k + 512 * (c + 1)],
                        start=(k == 0), stop=(k == DT - 1)),
                        reads=(), writes=(bPY[c],),
                        wait_reads=g_in if k == 0 else (),
                        rec_reads=g_in if k == DT - 1 else ())
                    em("tensor", lambda t, it=it, c=c, k=k, pu=pu, xo=xo:
                       t.matmul(
                           pu[:, :],
                           AR2b[:, II * k + 128 * it:II * k + 128 * (it + 1)],
                           AR3b[:, xo + 2048 * k + 512 * c:
                                xo + 2048 * k + 512 * (c + 1)],
                           start=(k == 0), stop=(k == DT - 1)),
                       reads=(), writes=(bu,),
                       wait_reads=u_in if k == 0 else (),
                       rec_reads=u_in if k == DT - 1 else ())
                em("scalar", lambda s, it=it, c=c: s.activation(
                    sgatev[c % 2], psY[:, 512 * c:512 * (c + 1)], AF.Silu,
                    scale=sgA_sb[:, it:it + 1]),
                    reads=(bPY[c], bsgA), writes=(bsgate[c % 2],))
                em("vector", lambda v, it=it, c=c, pu=pu:
                   v.scalar_tensor_tensor(
                       hrawcv[c % 2], pu[:, :],
                       suA_sb[:, it:it + 1], sgatev[c % 2], OP.mult, OP.mult),
                   reads=(bu, bsuA, bsgate[c % 2]), writes=(bhrawc[c % 2],))
                em("vector", lambda v, c=c: v.tensor_reduce(
                    stat_sb[:, 5:6], hrawcv[c % 2], AX.X,
                    apply_absolute_value=True, op=OP.max),
                    reads=(bhrawc[c % 2],), writes=(bstat,))
                em("vector", lambda v: v.tensor_tensor(
                    gmx_sb[:, 2:3], gmx_sb[:, 2:3], stat_sb[:, 5:6], OP.max),
                    reads=(bgmx, bstat), writes=(bgmx,))
                r0 = b * II + 128 * it
                bhspill[(b, it, c)] = buf(f"hsp{b}_{it}_{c}")
                stdma(hraw_d[r0:r0 + 128, 512 * c:512 * (c + 1)],
                      hrawcv[c % 2], bhrawc[c % 2], bhspill[(b, it, c)])
            if b == 0 and it < DT:
                chain_pipe(xq3, xq_units(1, range(2 * it, 2 * it + 2)),
                           "vector", "vector")

    em("gpsimd", lambda g: g.tensor_reduce(
        gmx_sb[0:1, 3:4], gmx_sb[:, 2:3], AX.C, OP.max),
        reads=(bgmx,), writes=(bgmx,))
    bcc2 = buf("cc2")
    stdma(cc2_in.ap(), gmx_sb[0:1, 3:4], bgmx, bcc2)
    em("gpsimd", lambda g: g.collective_compute(
        "AllReduce", OP.max, replica_groups=[list(range(NCORE))],
        ins=[cc2_in.ap().opt()], outs=[cc2_out.ap().opt()]),
        reads=(bcc2,), writes=(bcc2,))
    ld(gmx_sb[0:1, 3:4], cc2_out.ap(), bgmx, rbufs=(bcc2,), eng="gpsimd")
    em("tensor", lambda t: t.matmul(pb0[:, 0:1], ones_f[:, :],
                                    gmx_sb[0:1, 3:4], start=True, stop=True),
       reads=(bonef, bgmx), writes=(bPB[0],))
    em("scalar", lambda s: s.activation(gmx_sb[:, 3:4], pb0[:, 0:1],
                                        AF.Identity),
       reads=(bPB[0],), writes=(bgmx,))
    em("vector", lambda v: v.tensor_tensor(scal_sb[:, 4:5], sig_sb[:, 1:2],
                                           gmx_sb[:, 3:4], OP.mult),
       reads=(bsig, bgmx), writes=(bscal,))
    em("vector", lambda v: v.reciprocal(scal_sb[:, 5:6], scal_sb[:, 4:5]),
       reads=(bscal,), writes=(bscal,))
    em("vector", lambda v: v.tensor_scalar(scal_sb[:, 5:6], scal_sb[:, 5:6],
                                           127.0, None, OP.mult),
       reads=(bscal,), writes=(bscal,))
    em("vector", lambda v: v.tensor_scalar(scal_sb[:, 6:7], scal_sb[:, 4:5],
                                           float(np.float32(1.0 / 127.0)), None,
                                           OP.mult),
       reads=(bscal,), writes=(bscal,))
    bsdA = buf("sdA")
    em("vector", lambda v: v.tensor_scalar(sdA_sb[:, :], wds_sb[:, :],
                                           scal_sb[:, 6:7], None, OP.mult),
       reads=(bwds, bscal), writes=(bsdA,))

    p.fence()
    hq3 = mk_sets([(AR1, AR1i, 11264), (AR1, AR1i, 13824),
                   (AR2, AR2i, 5632), (AR2, AR2i, 8192),
                   (AR2, AR2i, 10752), (AR2, AR2i, 13312)], 512)
    hqc_par = [AR3b, AR2b]
    bhqc2 = [[buf(f"hqc{pp}_{i}") for i in range(IT)] for pp in range(2)]
    dnf = AR3.ap()[:, 5632:9728]
    bdnf = [buf(f"dnf{h}") for h in range(DT)]
    x1rv = [AR3.ap()[:, 9728:10752], AR3.ap()[:, 10752:11776]]
    outTv = [AR3.ap()[:, 11776:12800], AR3.ap()[:, 12800:13824]]
    bx1r = [buf("x1r0"), buf("x1r1")]
    boutT = [buf("outT0"), buf("outT1")]

    bwdT = buf("wdT")
    for it in range(IT):
        ld(AR1b[:, D * it:D * (it + 1)],
           bass.AP(wdT_d.tensor, 128 * it, [[DT * 2816, 128], [2816, DT],
                                            [1, 128]]), bwdT)

    hq_units = []
    for b in range(BLOC):
        for c in range(CH):
            cp = (b * CH + c) % 2
            for it in range(IT):
                src = bass.AP(hraw_d.tensor,
                              (b * II + 128 * it) * L + 512 * c,
                              [[L, 128], [1, 512]])
                hq_units.append(dict(
                    src=src, n=512, isc=(scal_sb[:, 5:6], bscal),
                    ldr=(bhspill[(b, it, c)],),
                    out=(lambda cp=cp, it=it: lambda scr, bscr: (
                        hqc_par[cp][:, 512 * it:512 * (it + 1)],
                        bhqc2[cp][it]))()))

    def emit_b2_mm(b, c):
            cp = (b * CH + c) % 2
            hqc = hqc_par[cp]
            bhqc = bhqc2[cp]
            DNB = [pb0, pb1, pc0, pc1]
            bDNB = [bPB[0], bPB[1], bPC[0], bPC[1]]
            d_in = (bwdT,) + tuple(bhqc)
            for hh2 in range(0, DT, 2):
                for it in range(IT):
                    for hh in (hh2, hh2 + 1):
                        pg = DNB[hh % 4]
                        bg = bDNB[hh % 4]
                        em("tensor", lambda t, hh=hh, it=it, pg=pg, hqc=hqc:
                           t.matmul(
                               pg[:, :],
                               AR1b[:, D * it + 128 * hh:
                                    D * it + 128 * (hh + 1)],
                               hqc[:, 512 * it:512 * (it + 1)],
                               start=(it == 0), stop=(it == IT - 1)),
                           reads=(), writes=(bg,),
                           wait_reads=d_in if hh2 == 0 and it == 0 and
                           hh == hh2 else (),
                           rec_reads=d_in if hh2 == DT - 2 and it == IT - 1
                           and hh == hh2 + 1 else ())
                for hh in (hh2, hh2 + 1):
                    pg = DNB[hh % 4]
                    bg = bDNB[hh % 4]
                    em("scalar", lambda s, hh=hh, pg=pg: s.activation(
                        dnf[:, 512 * hh:512 * (hh + 1)], pg[:, :], AF.Identity,
                        scale=sdA_sb[:, hh:hh + 1]),
                        reads=(bg, bsdA), writes=(bdnf[hh],))

    def emit_b2_out(b, c):
            for tb in range(4):
                tp = tb % 2
                row0 = b * L + 512 * c + 128 * tb
                ld(x1rv[tp], x1_d[row0:row0 + 128, :], bx1r[tp])
                for hh in range(DT):
                    pyc = (tb * DT + hh) % 4
                    em("tensor", lambda t, hh=hh, tb=tb, pyc=pyc: t.transpose(
                        psY[:, 512 * pyc:512 * pyc + 128],
                        dnf[:, 512 * hh + 128 * tb:512 * hh + 128 * (tb + 1)],
                        identf[:, :]), reads=(bdnf[hh], bid),
                        writes=(bPY[pyc],))
                    em("vector", lambda v, hh=hh, tp=tp, pyc=pyc:
                       v.tensor_tensor(
                           outTv[tp][:, 128 * hh:128 * (hh + 1)],
                           x1rv[tp][:, 128 * hh:128 * (hh + 1)],
                           psY[:, 512 * pyc:512 * pyc + 128],
                           OP.add), reads=(bx1r[tp], bPY[pyc]),
                       writes=(boutT[tp],))
                stdma(out_d[row0:row0 + 128, :], outTv[tp], boutT[tp],
                      buf("out_store"))

    def b2_after_tail(u):
        if (u + 1) % IT == 0:
            ch = u // IT
            emit_b2_mm(ch // CH, ch % CH)
        if u % IT == 16 and u // IT >= 1:
            ch = u // IT - 1
            emit_b2_out(ch // CH, ch % CH)

    chain_pipe(hq3, hq_units, "vector", "vector", after_tail=b2_after_tail)
    emit_b2_out((BLOC * CH - 1) // CH, (BLOC * CH - 1) % CH)

    fin = [(s, c) for s, c in p.cnt.items() if c > 0]
    em("gpsimd", lambda g: g.nop(), extra=fin)
    em("sync", lambda e: e.nop(), extra=fin)

    with nc.Block() as block:
        def emit_stream(eng_name):
            def _inner(eng):
                for wl, fn, sem, amt in p.stream[eng_name]:
                    for s, v in wl:
                        eng.wait_ge(sems[s], v)
                    instr = fn(eng)
                    if instr is not None:
                        instr.then_inc(sems[sem], amt)
            return _inner
        block.sync(emit_stream("sync"))
        block.vector(emit_stream("vector"))
        block.scalar(emit_stream("scalar"))
        block.tensor(emit_stream("tensor"))
        block.gpsimd(emit_stream("gpsimd"))

    st_.close()
    return nc


def kernel(L0_attn_x, L0_attn_A_log, L0_attn_W_dt, L0_attn_b_dt,
           L0_attn_W_xp, L0_attn_b_xp, L0_attn_W_out, L0_attn_b_out,
           L0_ffn_clip_x, L0_ffn_clip_down, L0_ffn_w_gate, L0_ffn_w_up,
           L0_ffn_w_down):
    x = np.ascontiguousarray(np.asarray(L0_attn_x, dtype=np.float32))
    A_log = np.asarray(L0_attn_A_log, dtype=np.float32)
    W_dt = np.asarray(L0_attn_W_dt, dtype=np.float32)
    b_dt = np.asarray(L0_attn_b_dt, dtype=np.float32)
    W_xp = np.ascontiguousarray(np.asarray(L0_attn_W_xp, dtype=np.float32))
    b_xp = np.asarray(L0_attn_b_xp, dtype=np.float32)
    W_out = np.ascontiguousarray(np.asarray(L0_attn_W_out, dtype=np.float32))
    b_out = np.asarray(L0_attn_b_out, dtype=np.float32)
    clip_x = np.asarray(L0_ffn_clip_x, dtype=np.float32)
    clip_down = np.asarray(L0_ffn_clip_down, dtype=np.float32)
    w_gate = np.ascontiguousarray(np.asarray(L0_ffn_w_gate, dtype=np.float32))
    w_up = np.ascontiguousarray(np.asarray(L0_ffn_w_up, dtype=np.float32))
    w_down = np.ascontiguousarray(np.asarray(L0_ffn_w_down, dtype=np.float32))

    A_img = np.ascontiguousarray(
        A_log.reshape(DT, 128, NS).transpose(1, 0, 2).reshape(128, DT * NS))
    wdt_img = np.ascontiguousarray(W_dt[:, 0].reshape(DT, 128).T)
    bdt_img = np.ascontiguousarray(b_dt.reshape(DT, 128).T)
    bout_img = np.ascontiguousarray(b_out.reshape(DT, 128).T)
    bxp_img = np.ascontiguousarray(b_xp.reshape(32, 1))
    clips_img = np.ascontiguousarray(
        np.array([[clip_x[0], clip_down[0]]], dtype=np.float32))

    nc = build_nc()
    shared = dict(A_img=A_img, wdt_img=wdt_img, bdt_img=bdt_img,
                  wxp=W_xp, bxp_img=bxp_img, wout=W_out, bout_img=bout_img,
                  clips_img=clips_img, w_gate=w_gate, w_up=w_up,
                  w_down=w_down)
    in_maps = []
    for c in range(NCORE):
        m = dict(shared)
        m["x"] = np.ascontiguousarray(
            x[BLOC * c:BLOC * (c + 1)].reshape(BLOC * L, D))
        in_maps.append(m)
    res = run_bass_kernel_spmd(nc, in_maps, core_ids=list(range(NCORE)))
    out = np.empty((16, L, D), dtype=np.float32)
    for c in range(NCORE):
        out[BLOC * c:BLOC * (c + 1)] = np.asarray(
            res.results[c]["out"]).reshape(BLOC, L, D)
    return out
